# version 03
# baseline (speedup 1.0000x reference)
import contextlib
import numpy as np
import concourse.bass as bass
import concourse.mybir as mybir
from concourse.bass_utils import run_bass_kernel_spmd
from concourse.alu_op_type import AluOpType as OP

F32 = mybir.dt.float32
BF16 = mybir.dt.bfloat16
U32 = mybir.dt.uint32
AF = mybir.ActivationFunctionType
AX = mybir.AxisListType

P = 128
NTOK = 4096
D = 1024
NT = NTOK // P
KC = D // P
LN_EPS = 1e-5
N_CORES = 8

ENGS = ("tensor", "vector", "scalar", "gpsimd", "sync")


def ceil_div(a, b):
    return (a + b - 1) // b



class Sem:
    def __init__(self, h, name):
        self.h = h
        self.name = name
        self.v = 0


def _flat(waits):
    out = []
    for w in waits:
        if w is None:
            continue
        if isinstance(w, list):
            out.extend(_flat(w))
        else:
            out.append(w)
    return out


class Ctx:

    def __init__(self, nc, sems, res=None):
        self.nc = nc
        self.sems = sems
        self.prog = {e: [] for e in ENGS}
        self.seen = {e: {} for e in ENGS}
        self.r = res
        self.sites = {}
        self.site_next = {q: lo for q, (lo, hi) in self.QPOOL.items()}

    def _prune(self, engname, waits):
        best = {}
        for sem, val in _flat(waits):
            if val <= 0:
                continue
            if best.get(sem.name, 0) < val:
                best[sem.name] = val
        out = []
        seen = self.seen[engname]
        for name, val in best.items():
            if seen.get(name, 0) >= val:
                continue
            seen[name] = val
            out.append((self.sems[name].h, val))
        return out

    def op(self, engname, fn, waits=(), inc=None, inck=1):
        pruned = self._prune(engname, waits)
        tok = None
        semh = None
        if inc is not None:
            sem = self.sems[inc]
            sem.v += inck
            tok = (sem, sem.v)
            semh = sem.h
        self.prog[engname].append((pruned, fn, semh, inck))
        return tok

    QPOOL = {"gpsimd": (0, 16), "sync": (16, 26), "scalar": (26, 32)}

    def new_phase(self):
        self.sites = {}
        self.site_next = {q: lo for q, (lo, hi) in self.QPOOL.items()}

    def dma(self, queue, site, fn, waits=()):
        key = (queue, site)
        if key not in self.sites:
            lo, hi = self.QPOOL[queue]
            nxt = self.site_next[queue]
            assert nxt < hi, f"dma site pool exhausted for {queue}"
            self.sites[key] = f"d{nxt:02d}"
            self.site_next[queue] = nxt + 1
        return self.op(queue, fn, waits=waits, inc=self.sites[key], inck=16)

    def dma_totals(self):
        return [(self.sems[f"d{i:02d}"], self.sems[f"d{i:02d}"].v) for i in range(32)]

    def barrier(self, engines=ENGS, drain_dma=True):
        waits = [(self.sems[n], self.sems[n].v) for n in ("t", "v", "s", "g")]
        if drain_dma:
            waits = waits + self.dma_totals()
        if not hasattr(self, "debug_barriers"):
            self.debug_barriers = []
        self.debug_barriers.append({s.name: v for s, v in waits if v > 0})
        for en in engines:
            self.op(en, lambda eng: None, waits=list(waits))
        self.new_phase()

    def finalize(self, block):
        for en in ENGS:
            sec = getattr(block, en)

            def body(eng, en=en):
                for waits, fn, semh, inck in self.prog[en]:
                    for semhh, val in waits:
                        eng.wait_ge(semhh, val)
                    try:
                        ins = fn(eng)
                    except Exception:
                        import traceback as _tb
                        _tb.print_exc()
                        raise
                    if semh is not None:
                        ins.then_inc(semh, inck)
            sec(body)


SEM_NAMES = [
    "t", "v", "s", "g",
    "b_tensor", "b_vector", "b_scalar", "b_gpsimd", "b_sync",
] + [f"d{i:02d}" for i in range(32)]



class Res:
    def __init__(self, nc, stack):
        sb = lambda name, shape, dt: stack.enter_context(nc.sbuf_tensor(name, shape, dt))
        self.hcmA = sb("hcmA", [P, KC, NTOK], BF16)
        self.hcmB = sb("hcmB", [P, KC, NTOK], BF16)
        self.scrf = sb("scrf", [P, 512], F32)
        self.xtile = sb("xtile", [P, 2, D], F32)
        self.rtile = sb("rtile", [P, 2, D], F32)
        self.htile = sb("htile", [P, 2, D], BF16)
        self.stats = sb("stats", [P, 64], F32)
        self.consts = sb("consts", [P, 16], F32)
        self.identb = sb("identb", [P, P], BF16)
        self.identf = sb("identf", [P, P], F32)
        self.onesb = sb("onesb", [P, 512], BF16)
        self.triub = sb("triub", [P, P], BF16)
        self.striu = sb("striu", [P, P], BF16)
        self.onesf = sb("onesf", [1, P], F32)
        self.vcol = sb("vcol", [P, 64], F32)
        self.vrow = None
        self.vrowf = None
        self.ps = [stack.enter_context(nc.psum_tensor(f"ps{i}", [P, 512], F32))
                   for i in range(6)]
        self.pt = [stack.enter_context(nc.psum_tensor(f"pt{i}", [P, 1024], BF16))
                   for i in range(2)]
        self.ps_read = [None] * 6
        self.ps_write = [None] * 6
        self.pt_read = [None] * 2
        self.pt_write = [None] * 2
        self.scrf_off = 0
        self.vrow_off = 0
        self.vrowf_off = 0
        self.vcol_off = 0

    _rows_ctr = [0]

    def open_rows(self, stack, nc, nbf, nf32=0):
        self._rows_ctr[0] += 1
        i = self._rows_ctr[0]
        self.vrow = stack.enter_context(nc.sbuf_tensor(f"rows_bf{i}", [1, nbf], BF16)) if nbf else None
        self.vrowf = stack.enter_context(nc.sbuf_tensor(f"rows_f32_{i}", [1, nf32], F32)) if nf32 else None
        self.vrow_n = nbf
        self.vrowf_n = nf32
        self.vrow_off = 0
        self.vrowf_off = 0

    def falloc(self, ncols):
        ap = self.scrf[:, self.scrf_off:self.scrf_off + ncols]
        assert self.scrf_off + ncols <= 512
        self.scrf_off += ncols
        return ap

    def ralloc(self, n):
        ap = self.vrow[:, self.vrow_off:self.vrow_off + n]
        assert self.vrow_off + n <= self.vrow_n
        self.vrow_off += n
        return ap

    def rfalloc(self, n):
        ap = self.vrowf[:, self.vrowf_off:self.vrowf_off + n]
        assert self.vrowf_off + n <= self.vrowf_n
        self.vrowf_off += n
        return ap

    def calloc(self, n):
        ap = self.vcol[:, self.vcol_off:self.vcol_off + n]
        assert self.vcol_off + n <= 64
        self.vcol_off += n
        return ap


def emit_setup(c):
    r = c.r
    t_ones = c.op("gpsimd", lambda g: g.memset(r.onesb[:], 1.0), inc="g")
    c.tok_eps = c.op("gpsimd", lambda g: g.memset(r.consts[:, 0:1], LN_EPS), inc="g")
    c.tok_eps8 = c.op("gpsimd", lambda g: g.memset(r.consts[:, 1:2], 1e-8), inc="g")
    c.tok_identb = c.op("gpsimd", lambda g: g.affine_select(
        out=r.identb[:], in_=r.onesb[:, 0:P], pattern=[[1, P]], base=0,
        channel_multiplier=-1, compare_op=OP.is_equal, fill=0.0),
        waits=[t_ones], inc="g")
    c.tok_identf = c.op("vector", lambda v: v.tensor_copy(
        out=r.identf[:], in_=r.identb[:]),
        waits=[c.tok_identb], inc="v")
    c.tok_triub = c.op("gpsimd", lambda g: g.affine_select(
        out=r.triub[:], in_=r.onesb[:, 0:P], pattern=[[1, P]], base=0,
        channel_multiplier=-1, compare_op=OP.is_ge, fill=0.0),
        waits=[t_ones], inc="g")
    c.tok_striu = c.op("gpsimd", lambda g: g.affine_select(
        out=r.striu[:], in_=r.onesb[:, 0:P], pattern=[[1, P]], base=-1,
        channel_multiplier=-1, compare_op=OP.is_ge, fill=0.0),
        waits=[t_ones], inc="g")
    c.tok_onesf = c.op("gpsimd", lambda g: g.memset(r.onesf[:, :], 1.0), inc="g")
    c.op("gpsimd", lambda g: g.memset(r.consts[:, 3:4], -3.465735902799726547), inc="g")
    c.tok_ln32 = (c.sems["g"], c.sems["g"].v)
    c.op("gpsimd", lambda g: g.memset(r.consts[:, 4:5], 1e-30), inc="g")
    c.tok_tiny = (c.sems["g"], c.sems["g"].v)
    c.op("gpsimd", lambda g: g.memset(r.consts[:, 5:6], 1e-24), inc="g")
    c.tok_t24 = (c.sems["g"], c.sems["g"].v)
    c.tok_ones = c.tok_identf



class WView:
    def __init__(self, ap, kchunks, ncols, tok):
        self.ap = ap
        self.kchunks = kchunks
        self.ncols = ncols
        self.tok = tok

    def sl(self, kc, n0, n1):
        return self.ap[:, kc, n0:n1]


def load_row_bf(c, dram_vec, n):
    ap = c.r.ralloc(n)
    tok = c.dma("gpsimd", "rows", lambda g: g.dma_start(out=ap, in_=dram_vec[None, :]))
    c.rows_tok = tok
    return ap, tok


def load_row_f32(c, dram_vec, n):
    ap = c.r.rfalloc(n)
    tok = c.dma("sync", "rowsf", lambda s: s.dma_start(out=ap, in_=dram_vec[None, :]))
    c.rowsf_tok = tok
    return ap, tok


def row_to_cols(c, row_ap, row_tok, n, bank=5):
    r = c.r
    a = n // P
    dst = r.calloc(a)
    ps = r.ps[bank]
    toks = []
    for i in range(a):
        tk = c.op("tensor", lambda t, i=i: t.transpose(
            ps[:, i:i + 1], row_ap[0:1, i * P:(i + 1) * P], r.identf[0:1, 0:1]),
            waits=[row_tok, c.tok_identf, c.ps_read[bank] if False else c.r.ps_read[bank]],
            inc="t")
        toks.append(tk)
    c.r.ps_write[bank] = toks[-1]
    tev = c.op("vector", lambda v: v.tensor_copy(out=dst, in_=ps[:, 0:a]),
               waits=[toks[-1]], inc="v")
    c.r.ps_read[bank] = tev
    return dst, tev



def emit_ln_prologue(c, xin, hcm, gamma_cols=None, beta_cols=None, gb_tok=None,
                     also_store=None, xin_tok=None, eps_idx=0, tile_cb=None):
    r = c.r
    toks = []
    xt_war = [None, None]
    ht_war = [None, None]
    stats_war = None
    for tt in range(NT):
        sl = tt % 2
        hs = tt % 2
        ptb = tt % 2
        xt = r.xtile[:, sl, :]
        ht = r.htile[:, hs, :]
        t_x = c.dma("scalar", f"lnx{sl}", lambda s, xt=xt, tt=tt: s.dma_start(
            out=xt, in_=xin[tt * P:(tt + 1) * P, :]),
            waits=[xt_war[sl], xin_tok])
        st = r.stats
        bn = st[:, 0:12]
        mv = st[:, 12:14]
        lnv = st[:, 14:15]
        rstd = st[:, 15:16]
        nb = st[:, 16:17]
        t1 = c.op("vector", lambda v, xt=xt, bn=bn: v.bn_stats(out=bn[:, 0:6], in_=xt[:, 0:512]),
                  waits=[t_x], inc="v")
        t2 = c.op("vector", lambda v, xt=xt, bn=bn: v.bn_stats(out=bn[:, 6:12], in_=xt[:, 512:1024]),
                  waits=[t_x], inc="v")
        t3 = c.op("vector", lambda v, bn=bn, mv=mv: v.bn_aggr(out=mv, in_=bn),
                  waits=[t1, t2], inc="v")
        t4 = c.op("scalar", lambda s, mv=mv, lnv=lnv: s.activation(
            out=lnv, in_=mv[:, 1:2], func=AF.Ln, bias=r.consts[:, eps_idx:eps_idx + 1], scale=1.0),
            waits=[t3, c.tok_eps, stats_war], inc="s")
        t5 = c.op("scalar", lambda s, lnv=lnv, rstd=rstd: s.activation(
            out=rstd, in_=lnv, func=AF.Exp, scale=-0.5),
            waits=[t4], inc="s")
        t6 = c.op("vector", lambda v, mv=mv, rstd=rstd, nb=nb: v.scalar_tensor_tensor(
            out=nb, in0=mv[:, 0:1], scalar=-1.0, in1=rstd,
            op0=OP.mult, op1=OP.mult),
            waits=[t5], inc="v")
        t7 = c.op("scalar", lambda s, xt=xt, ht=ht, rstd=rstd, nb=nb: s.activation(
            out=ht, in_=xt, func=AF.Identity, scale=rstd, bias=nb),
            waits=[t6, ht_war[hs]], inc="s")
        xt_war[sl] = t7
        ht_readers = []
        if also_store is not None:
            t_st = c.dma("gpsimd", f"hst{hs}", lambda g, ht=ht, tt=tt: g.dma_start(
                out=also_store[tt * P:(tt + 1) * P, :], in_=ht),
                waits=[t7])
            ht_readers.append(t_st)
        ptbank = r.pt[ptb]
        last = None
        for kc in range(KC):
            last = c.op("tensor", lambda t, ht=ht, kc=kc, ptbank=ptbank: t.transpose(
                ptbank[:, kc * P:(kc + 1) * P], ht[:, kc * P:(kc + 1) * P], r.identb[:]),
                waits=[t7, c.tok_identb, r.pt_read[ptb]], inc="t")
        r.pt_write[ptb] = last
        ht_readers.append(last)
        ht_war[hs] = ht_readers
        dst = hcm[:, :, tt * P:(tt + 1) * P]
        src = ptbank.rearrange("p (k n) -> p k n", k=KC)
        if gamma_cols is None:
            tev = c.op("vector", lambda v, dst=dst, src=src: v.tensor_copy(out=dst, in_=src),
                       waits=[last], inc="v")
        else:
            for kc in range(KC):
                tev = c.op("vector", lambda v, dst=dst, src=src, kc=kc: v.tensor_scalar(
                    out=dst[:, kc, :], in0=src[:, kc, :],
                    scalar1=gamma_cols[:, kc:kc + 1], scalar2=beta_cols[:, kc:kc + 1],
                    op0=OP.mult, op1=OP.add),
                    waits=[last, gb_tok], inc="v")
        r.pt_read[ptb] = tev
        if tile_cb is not None:
            cbrec = {"tt": tt, "xt": xt, "ht": ht, "ptb": ptb, "t_ht": t7,
                     "t_hcm": tev, "rstd": rstd, "nb": nb}
            extra = tile_cb(cbrec)
            if extra is not None:
                ht_war[hs] = [ht_war[hs], extra.get("ht_war")]
                xt_war[sl] = [xt_war[sl], extra.get("xt_war")]
                stats_war = extra.get("stats_war")
        toks.append(tev)
    return toks


def emit_ln_stream(c, xin, xout_dram, xin_tok=None):
    r = c.r
    xt_war = [None, None]
    ot_war = [None, None]
    toks = []
    for tt in range(NT):
        sl = tt % 2
        xt = r.xtile[:, sl, :]
        ot = r.rtile[:, sl, :]
        t_x = c.dma("scalar", f"lnsx{sl}", lambda s, xt=xt, tt=tt: s.dma_start(
            out=xt, in_=xin[tt * P:(tt + 1) * P, :]),
            waits=[xt_war[sl], xin_tok])
        st = r.stats
        bn = st[:, 0:12]
        mv = st[:, 12:14]
        t1 = c.op("vector", lambda v, xt=xt, bn=bn: v.bn_stats(out=bn[:, 0:6], in_=xt[:, 0:512]),
                  waits=[t_x], inc="v")
        t2 = c.op("vector", lambda v, xt=xt, bn=bn: v.bn_stats(out=bn[:, 6:12], in_=xt[:, 512:1024]),
                  waits=[t_x], inc="v")
        t3 = c.op("vector", lambda v, bn=bn, mv=mv: v.bn_aggr(out=mv, in_=bn),
                  waits=[t1, t2], inc="v")
        t4 = c.op("scalar", lambda s, mv=mv: s.activation(
            out=st[:, 14:15], in_=mv[:, 1:2], func=AF.Ln, bias=r.consts[:, 0:1], scale=1.0),
            waits=[t3, c.tok_eps], inc="s")
        t5 = c.op("scalar", lambda s: s.activation(
            out=st[:, 15:16], in_=st[:, 14:15], func=AF.Exp, scale=-0.5),
            waits=[t4], inc="s")
        t6 = c.op("vector", lambda v, mv=mv: v.scalar_tensor_tensor(
            out=st[:, 16:17], in0=mv[:, 0:1], scalar=-1.0, in1=st[:, 15:16],
            op0=OP.mult, op1=OP.mult),
            waits=[t5], inc="v")
        t7 = c.op("scalar", lambda s, xt=xt, ot=ot: s.activation(
            out=ot, in_=xt, func=AF.Identity, scale=st[:, 15:16], bias=st[:, 16:17]),
            waits=[t6, ot_war[sl]], inc="s")
        xt_war[sl] = t7
        t_o = c.dma("sync", f"lnso{sl}", lambda s, ot=ot, tt=tt: s.dma_start(
            out=xout_dram[tt * P:(tt + 1) * P, :], in_=ot),
            waits=[t7])
        ot_war[sl] = t_o
        toks.append(t_o)
    return toks



ESEM = {"tensor": "t", "vector": "v", "scalar": "s", "gpsimd": "g"}
MM_INC_ALL = [False]


def mm_accum(c, bank, parts, waits=(), n_out=None, m_base=0):
    r = c.r
    n = len(parts)
    last = None
    for i, (lh, rh) in enumerate(parts):
        m = lh.shape[-1]
        nn = n_out if n_out is not None else rh.shape[-1]
        tgt = r.ps[bank][m_base:m_base + m, 0:nn]
        w = (list(_flat(waits)) + [r.ps_read[bank]]) if i == 0 else ()
        last = c.op("tensor", lambda t, lh=lh, rh=rh, i=i, n=n, tgt=tgt: t.matmul(
            tgt, lh, rh, start=(i == 0), stop=(i == n - 1)),
            waits=w, inc="t" if (i == n - 1 or MM_INC_ALL[0]) else None)
    r.ps_write[bank] = last
    return last


def evac(c, bank, fn, eng="scalar", waits=()):
    r = c.r
    tok = c.op(eng, fn, waits=[r.ps_write[bank]] + list(_flat(waits)), inc=ESEM[eng])
    r.ps_read[bank] = tok
    return tok


def bias_part(c, row_ap, m0, m1, ncols):
    return (row_ap[0:1, m0:m1], c.r.onesb[0:1, 0:ncols])


def bias_free(c, row_ap, n0, n1, mrows):
    return (c.r.onesb[0:1, 0:mrows], row_ap[0:1, n0:n1])



def emit_l0_ffn(c, stack, xin, xout, W, hcm, hcm_toks, big):
    r = c.r
    nc = c.nc
    sb = lambda name, shape, dt: stack.enter_context(nc.sbuf_tensor(name, shape, dt))
    r.open_rows(stack, nc, 5120)

    u1buf = sb("s2_u1", [P, KC, 256], BF16)
    v2buf = sb("s2_v2", [P, 2, 1024], BF16)
    tU1 = c.dma("gpsimd", "wl_u1", lambda g: g.dma_start(
        out=u1buf[:, :, :], in_=W["U1"].rearrange("(k p) n -> p k n", p=P)))
    tV2 = c.dma("gpsimd", "wl_v2", lambda g: g.dma_start(
        out=v2buf[:, :, :], in_=W["V2"].rearrange("(k p) n -> p k n", p=P)))
    wU1 = WView(u1buf, KC, 256, tU1)
    wV2 = WView(v2buf, 2, 1024, tV2)
    bV1 = big[:, 0:2, 0:4096]
    bU2 = big[:, 2:4, 0:4096].rearrange("p a n -> p (a n)").rearrange("p (k n) -> p k n", k=32)
    bH = big[:, 4:8, 0:4096].rearrange("p a n -> p (a n)").rearrange("p (k n) -> p k n", k=32)
    tV1 = c.dma("gpsimd", "wl_v1", lambda g: g.dma_start(
        out=bV1, in_=W["V1"].rearrange("(k p) n -> p k n", p=P)))
    tU2 = c.dma("gpsimd", "wl_u2", lambda g: g.dma_start(
        out=bU2, in_=W["U2"].rearrange("(k p) n -> p k n", p=P)))
    b1row, _ = load_row_bf(c, W["b1"], 4096)
    b2row, _ = load_row_bf(c, W["b2"], 1024)
    t_b1 = t_b2 = c.rows_tok

    Psb = sb("s2_P", [P, 2, 512], BF16)
    Qsb = sb("s2_Q", [P, 2, 512], BF16)

    TQ = 512
    rr = [0]

    def nbank():
        b = rr[0] % 4
        rr[0] += 1
        return b

    war_P = [None, None]
    war_H = None
    war_Q = [None, None]
    res_war = [None, None]
    out_war = [None, None]
    for tc in range(NTOK // TQ):
        t0 = tc * TQ
        h_toks = [hcm_toks[i] for i in range(t0 // P, (t0 + TQ) // P)]
        pToks = []
        for m in range(2):
            bank = nbank()
            mm_accum(c, bank,
                     [(wU1.sl(kc, m * 128, (m + 1) * 128), hcm[:, kc, t0:t0 + TQ])
                      for kc in range(KC)],
                     waits=[wU1.tok] + h_toks)
            pToks.append(evac(c, bank, lambda s, m=m, bank=bank: s.copy(
                out=Psb[:, m, :], in_=r.ps[bank][:, :]), waits=[war_P[m]]))
        hToks = []
        last_H_mm = None
        for fm in range(32):
            bank = nbank()
            parts = [(bV1[:, rc, fm * 128:(fm + 1) * 128], Psb[:, rc, :]) for rc in range(2)]
            parts.append(bias_part(c, b1row, fm * 128, (fm + 1) * 128, TQ))
            last_H_mm = mm_accum(c, bank, parts,
                                 waits=[tV1, t_b1, c.tok_ones, war_H] + pToks)
            hToks.append(evac(c, bank, lambda s, fm=fm, bank=bank: s.activation(
                out=bH[:, fm, :], in_=r.ps[bank][:, :], func=AF.Gelu_apprx_tanh)))
        war_P = [last_H_mm, last_H_mm]
        qToks = []
        last_Q_mm = None
        for m in range(2):
            bank = 4 + m
            last_Q_mm = mm_accum(c, bank,
                                 [(bU2[:, fc, m * 128:(m + 1) * 128], bH[:, fc, :])
                                  for fc in range(32)],
                                 waits=[tU2, war_Q[m]] + hToks)
            qToks.append(evac(c, bank, lambda s, m=m, bank=bank: s.copy(
                out=Qsb[:, m, :], in_=r.ps[bank][:, :])))
        war_H = last_Q_mm
        last_O_mm = None
        for tt4 in range(TQ // P):
            tt = (t0 // P) + tt4
            rsl = tt % 2
            rt = r.rtile[:, rsl, :]
            t_rx = c.dma("sync", f"res{rsl}", lambda s, rt=rt, tt=tt: s.dma_start(
                out=rt, in_=xin[tt * P:(tt + 1) * P, :]),
                waits=[res_war[rsl]])
            oToks = []
            for eh in range(2):
                bank = nbank()
                parts = [(Qsb[:, rc, tt4 * 128:(tt4 + 1) * 128],
                          wV2.sl(rc, eh * 512, (eh + 1) * 512)) for rc in range(2)]
                parts.append(bias_free(c, b2row, eh * 512, (eh + 1) * 512, P))
                last_O_mm = mm_accum(c, bank, parts,
                                     waits=[wV2.tok, t_b2, c.tok_ones] + qToks)
                oToks.append(evac(c, bank, lambda v, eh=eh, rt=rt, bank=bank:
                                  v.tensor_tensor(out=rt[:, eh * 512:(eh + 1) * 512],
                                                  in0=r.ps[bank][:, :],
                                                  in1=rt[:, eh * 512:(eh + 1) * 512],
                                                  op=OP.add),
                                  eng="vector", waits=[t_rx]))
            out_war[rsl] = c.dma("sync", f"xo{rsl}", lambda s, rt=rt, tt=tt: s.dma_start(
                out=xout[tt * P:(tt + 1) * P, :], in_=rt),
                waits=oToks)
            res_war[rsl] = out_war[rsl]
        war_Q = [last_O_mm, last_O_mm]



def build_sublayer_program(emit_body, weight_specs):
    nc = bass.Bass("TRN2", target_bir_lowering=False, debug=False)
    xin = nc.dram_tensor("x0", [NTOK, D], F32, kind="ExternalInput").ap()
    xout = nc.dram_tensor("y", [NTOK, D], F32, kind="ExternalOutput").ap()
    W = {name: nc.dram_tensor(name, list(shape), F32, kind="ExternalInput").ap()
         for name, shape in weight_specs}
    with contextlib.ExitStack() as stack:
        sems = {n: Sem(stack.enter_context(nc.semaphore(n)), n) for n in SEM_NAMES}
        r = Res(nc, stack)
        c = Ctx(nc, sems, r)
        emit_setup(c)
        if getattr(emit_body, "needs_prologue", True):
            hcm_toks = emit_ln_prologue(c, xin, r.hcmA)
        else:
            hcm_toks = None
        emit_body(c, stack, xin, xout, W, r.hcmA, hcm_toks, r.hcmB)
        c.op("sync", lambda s: None, waits=c.dma_totals())
        with nc.Block() as block:
            c.finalize(block)
    return nc



def _elu_evac(c, bank, dst, tmp_exp, waits=(), war=None):
    r = c.r
    n = dst.shape[-1]
    prev = getattr(c, "_elu_tmp_war", None)
    t1 = c.op("scalar", lambda s: s.activation(out=tmp_exp[:, 0:n], in_=r.ps[bank][:, 0:n],
                                               func=AF.Exp),
              waits=[r.ps_write[bank], prev] + list(_flat(waits)), inc="s")
    t2 = c.op("vector", lambda v: v.tensor_scalar(out=tmp_exp[:, 0:n], in0=tmp_exp[:, 0:n],
                                                  scalar1=-1.0, scalar2=0.0,
                                                  op0=OP.add, op1=OP.min),
              waits=[t1], inc="v")
    t3 = c.op("vector", lambda v: v.scalar_tensor_tensor(out=dst, in0=r.ps[bank][:, 0:n],
                                                         scalar=0.0, in1=tmp_exp[:, 0:n],
                                                         op0=OP.max, op1=OP.add),
              waits=[t2, war], inc="v")
    c._elu_tmp_war = t3
    r.ps_read[bank] = [t1, t3]
    return t3


def emit_l0_attn_a(c, stack, xin, xout, W, hcm, hcm_toks, big):
    r = c.r
    nc = c.nc
    H = 8
    sb = lambda name, shape, dt: stack.enter_context(nc.sbuf_tensor(name, shape, dt))
    r.open_rows(stack, nc, 6144)
    battn = big
    aout = big[:, :, :].rearrange("p a n -> p (a n)").rearrange("p (a n) -> p a n", a=NT)

    bqrow, _ = load_row_bf(c, W["bq"], D)
    bkrow, _ = load_row_bf(c, W["bk"], D)
    bvrow, _ = load_row_bf(c, W["bv"], D)
    burow, _ = load_row_bf(c, W["bu"], D)
    borow, _ = load_row_bf(c, W["bo"], D)
    scrow, _ = load_row_bf(c, W["scale"], D)
    t_bq = t_bk = t_bv = t_bu = t_bo = t_sc = c.rows_tok

    with contextlib.ExitStack() as ph1:
        sb1 = lambda name, shape, dt: ph1.enter_context(nc.sbuf_tensor(name, shape, dt))
        wq = sb1("s1_wq", [P, KC, P], BF16)
        wk = sb1("s1_wk", [P, KC, P], BF16)
        wv = sb1("s1_wv", [P, KC, P], BF16)
        qh = sb1("s1_q", [P, NTOK], BF16)
        kh = sb1("s1_k", [P, NTOK], BF16)
        vh = sb1("s1_v", [P, NT, P], BF16)
        ktm = sb1("s1_ktm", [P, 2, P], BF16)
        Et = sb1("s1_E", [P, 2, P], BF16)
        Ssb = sb1("s1_S", [P, P], BF16)
        tmpx = sb1("s1_tmp", [P, 512], BF16)
        w_war = [None, None, None]
        STATE = 5
        for h in range(H):
            hs = slice(h * P, (h + 1) * P)
            t_wq = c.dma("gpsimd", "wl_wq", lambda g, h=h: g.dma_start(
                out=wq[:, :, :], in_=W["WqT"][:, h * P:(h + 1) * P].rearrange("(k p) n -> p k n", p=P)),
                waits=[w_war[0]])
            t_wk = c.dma("gpsimd", "wl_wk", lambda g, h=h: g.dma_start(
                out=wk[:, :, :], in_=W["WkT"][:, h * P:(h + 1) * P].rearrange("(k p) n -> p k n", p=P)),
                waits=[w_war[1]])
            t_wv = c.dma("gpsimd", "wl_wv", lambda g, h=h: g.dma_start(
                out=wv[:, :, :], in_=W["WvT"][:, h * P:(h + 1) * P].rearrange("(k p) n -> p k n", p=P)),
                waits=[w_war[2]])
            attn_done = getattr(c, "_s1_attn_done", None)
            q_toks = []
            k_toks = []
            v_toks = []
            for ncol in range(8):
                nsl = slice(ncol * 512, (ncol + 1) * 512)
                bank = ncol % 2
                mm_accum(c, bank,
                         [(wq[:, kc, :], hcm[:, kc, nsl]) for kc in range(KC)] +
                         [bias_part(c, bqrow, h * P, (h + 1) * P, 512)],
                         waits=[t_wq, t_bq, c.tok_ones] + hcm_toks)
                q_toks.append(_elu_evac(c, bank, qh[:, nsl], tmpx, war=attn_done))
                bank = 2 + ncol % 2
                mm_accum(c, bank,
                         [(wk[:, kc, :], hcm[:, kc, nsl]) for kc in range(KC)] +
                         [bias_part(c, bkrow, h * P, (h + 1) * P, 512)],
                         waits=[t_wk, t_bk, c.tok_ones] + hcm_toks)
                k_toks.append(_elu_evac(c, bank, kh[:, nsl], tmpx, war=attn_done))
            for tt in range(NT):
                bank = 4
                mm_accum(c, bank,
                         [(hcm[:, kc, tt * P:(tt + 1) * P], wv[:, kc, :]) for kc in range(KC)] +
                         [bias_free(c, bvrow, h * P, (h + 1) * P, P)],
                         waits=[t_wv, t_bv, c.tok_ones] + hcm_toks)
                v_toks.append(evac(c, bank, lambda s, tt=tt, bank=bank: s.activation(
                    out=vh[:, tt, :], in_=r.ps[bank][:, 0:P], func=AF.Silu),
                    waits=[attn_done]))
            t_state_mm = None
            t_scopy = None
            for n in range(NT):
                nsl = slice(n * P, (n + 1) * P)
                ob = 2 + (n % 2)
                eb = n % 2
                mm_accum(c, eb, [(kh[:, nsl], qh[:, nsl])],
                         waits=[k_toks[n // 4], q_toks[n // 4]])
                tE = evac(c, eb, lambda v, n=n, eb=eb: v.tensor_tensor(
                    out=Et[:, n % 2, :], in0=r.ps[eb][:, 0:P], in1=r.triub[:],
                    op=OP.mult), eng="vector", waits=[c.tok_triub])
                if n > 0:
                    t_scopy = c.op("scalar", lambda s: s.copy(out=Ssb[:], in_=r.ps[STATE][:, 0:P]),
                                   waits=[t_state_mm], inc="s")
                parts = []
                if n > 0:
                    parts.append((qh[:, nsl], Ssb[:]))
                parts.append((Et[:, n % 2, :], vh[:, n, :]))
                w0 = [tE, v_toks[n], q_toks[n // 4]]
                if n > 0:
                    w0.append(t_scopy)
                mm_accum(c, ob, parts, waits=w0)
                c.op("scalar", lambda s, n=n, h=h, ob=ob: s.copy(
                    out=aout[:, n, h * P:(h + 1) * P], in_=r.ps[ob][:, 0:P]),
                    waits=[r.ps_write[ob]], inc="s")
                r.ps_read[ob] = c.prog and (c.sems["s"], c.sems["s"].v)
                ptb = n % 2
                tk1 = c.op("tensor", lambda t, n=n, ptb=ptb, nsl=nsl: t.transpose(
                    r.pt[ptb][:, 0:P], kh[:, nsl], r.identb[:]),
                    waits=[k_toks[n // 4], c.tok_identb, r.pt_read[ptb]], inc="t")
                r.pt_write[ptb] = tk1
                tk2 = c.op("vector", lambda v, n=n, ptb=ptb: v.tensor_copy(
                    out=ktm[:, n % 2, :], in_=r.pt[ptb][:, 0:P]),
                    waits=[tk1], inc="v")
                r.pt_read[ptb] = tk2
                w1 = [tk2, v_toks[n]]
                if n > 0:
                    w1.append(t_scopy)
                else:
                    w1.append(r.ps_read[STATE])
                t_state_mm = c.op("tensor", lambda t, n=n: t.matmul(
                    r.ps[STATE][:, 0:P], ktm[:, n % 2, :], vh[:, n, :],
                    start=(n == 0), stop=True, skip_group_check=True),
                    waits=w1, inc="t")
                r.ps_write[STATE] = t_state_mm
            r.ps_read[STATE] = t_scopy
            c._s1_attn_done = t_state_mm
            w_war = [q_toks[-1], k_toks[-1], [v_toks[-1], t_state_mm]]
        last_aout = (c.sems["s"], c.sems["s"].v)
    c.barrier()
    if globals().get("S1_SKIP_EPILOGUE"):
        return
    with contextlib.ExitStack() as ph2:
        sb2 = lambda name, shape, dt: ph2.enter_context(nc.sbuf_tensor(name, shape, dt))
        wu = sb2("s1_wu", [P, KC, D], BF16)
        wo = sb2("s1_wo", [P, KC, D], BF16)
        ub = sb2("s1_u", [P, 1, D], BF16)
        gb = sb2("s1_g", [P, 2, D], BF16)
        gcm = sb2("s1_gcm", [P, 1, KC, P], BF16)
        scbc = sb2("s1_scbc", [P, D], BF16)
        t_wu = c.dma("gpsimd", "wl_wu", lambda g: g.dma_start(
            out=wu[:, :, :], in_=W["WuT"].rearrange("(k p) n -> p k n", p=P)))
        t_wo = c.dma("gpsimd", "wl_wo", lambda g: g.dma_start(
            out=wo[:, :, :], in_=W["WoT"].rearrange("(k p) n -> p k n", p=P)))
        for half in range(2):
            bank = 4 + half
            mm_accum(c, bank, [(r.onesb[0:1, 0:P], scrow[0:1, half * 512:(half + 1) * 512])],
                     waits=[t_sc, c.tok_ones])
            evac(c, bank, lambda s, half=half, bank=bank: s.copy(
                out=scbc[:, half * 512:(half + 1) * 512], in_=r.ps[bank][:, :]))
        t_scbc = (c.sems["s"], c.sems["s"].v)
        u_war = [None, None]
        g_war = [None, None]
        res_war = [None, None]
        out_war = [None, None]
        st = r.stats
        for tt in range(NT):
            sl2 = tt % 2
            uToks = []
            for half in range(2):
                bank = half
                mm_accum(c, bank,
                         [(hcm[:, kc, tt * P:(tt + 1) * P], wu[:, kc, half * 512:(half + 1) * 512])
                          for kc in range(KC)] +
                         [bias_free(c, burow, half * 512, (half + 1) * 512, P)],
                         waits=[t_wu, t_bu, c.tok_ones] + hcm_toks)
                uToks.append(evac(c, bank, lambda s, half=half, bank=bank: s.activation(
                    out=ub[:, 0, half * 512:(half + 1) * 512], in_=r.ps[bank][:, :],
                    func=AF.Silu), waits=[u_war[0]]))
            if globals().get("S1_EPILOGUE_LEVEL", 5) <= 1:
                continue
            at = aout[:, tt, :]
            ss = st[:, 20:21]
            t_sq = c.op("scalar", lambda s, at=at, ss=ss, sl2=sl2: s.activation(
                out=gb[:, sl2, :], in_=at, func=AF.Square, accum_out=ss),
                waits=[last_aout, g_war[sl2]], inc="s")
            t_ln = c.op("scalar", lambda s, ss=ss: s.activation(
                out=st[:, 21:22], in_=ss, func=AF.Ln, bias=r.consts[:, 4:5]),
                waits=[t_sq, c.tok_tiny], inc="s")
            t_rm = c.op("scalar", lambda s: s.activation(
                out=st[:, 22:23], in_=st[:, 21:22], func=AF.Exp, scale=0.5,
                bias=r.consts[:, 3:4]),
                waits=[t_ln, c.tok_ln32], inc="s")
            if globals().get("S1_EPILOGUE_LEVEL", 5) <= 2:
                continue
            t_ad = c.op("vector", lambda v: v.tensor_scalar(
                out=st[:, 23:24], in0=st[:, 22:23], scalar1=1e-8, scalar2=None, op0=OP.add),
                waits=[t_rm], inc="v")
            t_rc = c.op("vector", lambda v: v.reciprocal(out=st[:, 24:25], in_=st[:, 23:24]),
                        waits=[t_ad], inc="v")
            t_g1 = c.op("vector", lambda v, at=at, sl2=sl2: v.tensor_scalar(
                out=gb[:, sl2, :], in0=at, scalar1=st[:, 24:25], scalar2=None, op0=OP.mult),
                waits=[t_rc, g_war[sl2]], inc="v")
            t_g2 = c.op("vector", lambda v, sl2=sl2: v.tensor_tensor(
                out=gb[:, sl2, :], in0=gb[:, sl2, :], in1=scbc[:], op=OP.mult),
                waits=[t_g1, t_scbc], inc="v")
            t_g3 = c.op("vector", lambda v, sl2=sl2: v.tensor_tensor(
                out=gb[:, sl2, :], in0=gb[:, sl2, :], in1=ub[:, 0, :], op=OP.mult),
                waits=[t_g2] + uToks, inc="v")
            u_war[0] = t_g3
            if globals().get("S1_EPILOGUE_LEVEL", 5) <= 3:
                continue
            ptb = tt % 2
            tlast = None
            for kc in range(KC):
                tlast = c.op("tensor", lambda t, kc=kc, ptb=ptb, sl2=sl2: t.transpose(
                    r.pt[ptb][:, kc * P:(kc + 1) * P], gb[:, sl2, kc * P:(kc + 1) * P],
                    r.identb[:]),
                    waits=[t_g3, c.tok_identb, r.pt_read[ptb]], inc="t")
            r.pt_write[ptb] = tlast
            g_war[sl2] = tlast
            t_gc = c.op("vector", lambda v, ptb=ptb: v.tensor_copy(
                out=gcm[:, 0, :, :], in_=r.pt[ptb].rearrange("p (k n) -> p k n", k=KC)),
                waits=[tlast, getattr(c, "_s1_gcm_war", None)], inc="v")
            r.pt_read[ptb] = t_gc
            rt = r.rtile[:, sl2, :]
            t_rx = c.dma("sync", f"res{sl2}", lambda s, rt=rt, tt=tt: s.dma_start(
                out=rt, in_=xin[tt * P:(tt + 1) * P, :]),
                waits=[res_war[sl2]])
            oToks = []
            for half in range(2):
                bank = 2 + half
                lw = mm_accum(c, bank,
                              [(gcm[:, 0, kc, :], wo[:, kc, half * 512:(half + 1) * 512])
                               for kc in range(KC)] +
                              [bias_free(c, borow, half * 512, (half + 1) * 512, P)],
                              waits=[t_wo, t_bo, c.tok_ones, t_gc])
                c._s1_gcm_war = lw
                oToks.append(evac(c, bank, lambda v, half=half, rt=rt, bank=bank:
                                  v.tensor_tensor(out=rt[:, half * 512:(half + 1) * 512],
                                                  in0=r.ps[bank][:, :],
                                                  in1=rt[:, half * 512:(half + 1) * 512],
                                                  op=OP.add),
                                  eng="vector", waits=[t_rx]))
            out_war[sl2] = c.dma("sync", f"xo{sl2}", lambda s, rt=rt, tt=tt: s.dma_start(
                out=xout[tt * P:(tt + 1) * P, :], in_=rt),
                waits=oToks)
            res_war[sl2] = out_war[sl2]



def emit_l0_attn_b(c, stack, xin, xout, W, hcm, hcm_toks, big):
    r = c.r
    nc = c.nc
    sb = lambda name, shape, dt: stack.enter_context(nc.sbuf_tensor(name, shape, dt))
    with contextlib.ExitStack() as ph1:
        sb1 = lambda name, shape, dt: ph1.enter_context(nc.sbuf_tensor(name, shape, dt))
        wq = sb1("s3_wq", [P, KC, P], BF16)
        wk = sb1("s3_wk", [P, KC, P], BF16)
        wv = sb1("s3_wv", [P, KC, P], BF16)
        k2 = sb1("s3_k", [P, NTOK], BF16)
        q2 = sb1("s3_q", [P, 2, 512], BF16)
        vx = sb1("s3_v", [P, NT, 130], BF16)
        Pt = sb1("s3_P", [P, 4, 512], BF16)
        rs = sb1("s3_rs", [1, 2, 512], BF16)
        rsf = sb1("s3_rsf", [1, 2, 512], F32)
        bc = sb1("s3_bc", [64, 2, 512], BF16)
        t_vones = c.op("gpsimd", lambda g: g.memset(
            vx[:, :, :].rearrange("p a (b e) -> p a b e", e=65)[:, :, :, 64:65], 1.0),
            inc="g")
        w_war = [None, None, None]
        for pr in range(8):
            csl = slice(pr * P, (pr + 1) * P)
            attn_done = getattr(c, "_s3_attn_done", None)
            t_wq = c.dma("gpsimd", "wl_q", lambda g, csl=csl: g.dma_start(
                out=wq[:, :, :], in_=W["WqkvT"][:, csl].rearrange("(k p) n -> p k n", p=P)),
                waits=[w_war[0]])
            t_wk = c.dma("gpsimd", "wl_k", lambda g, csl=csl: g.dma_start(
                out=wk[:, :, :], in_=W["WqkvT"][:, 1024 + csl.start:1024 + csl.stop].rearrange("(k p) n -> p k n", p=P)),
                waits=[w_war[1]])
            t_wv = c.dma("gpsimd", "wl_v", lambda g, csl=csl: g.dma_start(
                out=wv[:, :, :], in_=W["WqkvT"][:, 2048 + csl.start:2048 + csl.stop].rearrange("(k p) n -> p k n", p=P)),
                waits=[w_war[2]])
            k_toks = []
            for ncol in range(8):
                nsl = slice(ncol * 512, (ncol + 1) * 512)
                bank = 4 + ncol % 2
                mm_accum(c, bank, [(wk[:, kc, :], hcm[:, kc, nsl]) for kc in range(KC)],
                         waits=[t_wk, attn_done] + hcm_toks)
                k_toks.append(evac(c, bank, lambda s, nsl=nsl, bank=bank: s.copy(
                    out=k2[:, nsl], in_=r.ps[bank][:, :]), waits=[attn_done]))
            v_toks = []
            for tt in range(NT):
                bank = 4 + tt % 2
                mm_accum(c, bank, [(hcm[:, kc, tt * P:(tt + 1) * P], wv[:, kc, :])
                                   for kc in range(KC)],
                         waits=[t_wv, attn_done] + hcm_toks)
                dst = vx[:, tt, :].rearrange("p (b e) -> p b e", e=65)[:, :, 0:64]
                v_toks.append(evac(c, bank, lambda s, dst=dst, bank=bank: s.copy(
                    out=dst, in_=r.ps[bank][:, 0:P].rearrange("p (b e) -> p b e", e=64)),
                    waits=[t_vones, attn_done]))
            pv_done = [None, None]
            pt_war = [None] * 4
            q_war = [None, None]
            for qc in range(8):
                qsl = slice(qc * 512, (qc + 1) * 512)
                qslot = qc % 2
                bank = 4 + qslot
                mm_accum(c, bank, [(wq[:, kc, :], hcm[:, kc, qsl]) for kc in range(KC)],
                         waits=[t_wq, q_war[qslot]] + hcm_toks)
                t_q = evac(c, bank, lambda s, qslot=qslot, bank=bank: s.activation(
                    out=q2[:, qslot, :], in_=r.ps[bank][:, :], func=AF.Identity, scale=0.125))
                nm = 4 * qc + 4
                pvA_last = None
                pvB_last = None
                for mi in range(nm):
                    eb = mi % 2
                    pslot = mi % 4
                    msl = slice(mi * P, (mi + 1) * P)
                    diag = (mi * P) >= qc * 512
                    for hh in range(2):
                        p0 = hh * 64
                        bE = (2 * mi + hh) % 2
                        mm_accum(c, bE, [(k2[p0:p0 + 64, msl], q2[p0:p0 + 64, qslot, :])],
                                 waits=[k_toks[mi // 4], t_q])
                        pslot2 = (2 * mi + hh) % 4
                        tE = evac(c, bE, lambda s, pslot2=pslot2, bE=bE: s.activation(
                            out=Pt[:, pslot2, :], in_=r.ps[bE][:, :], func=AF.Exp),
                            waits=[pt_war[pslot2]])
                        if diag:
                            tE = c.op("gpsimd", lambda g, pslot2=pslot2, base=qc * 512 - mi * P: g.affine_select(
                                out=Pt[:, pslot2, :], in_=Pt[:, pslot2, :], pattern=[[1, 512]],
                                base=base, channel_multiplier=-1, compare_op=OP.is_ge, fill=0.0),
                                waits=[tE], inc="g")
                        pvb = 2 + hh
                        vsl = vx[:, mi, hh * 65:(hh + 1) * 65]
                        t_pv = c.op("tensor", lambda t, vsl=vsl, pslot2=pslot2, pvb=pvb, mi=mi, nm=nm: t.matmul(
                            r.ps[pvb][0:65, :], vsl, Pt[:, pslot2, :],
                            start=(mi == 0), stop=(mi == nm - 1), skip_group_check=True),
                            waits=[tE, v_toks[mi], r.ps_read[pvb] if mi == 0 else None], inc="t")
                        r.ps_write[pvb] = t_pv
                        pt_war[pslot2] = t_pv
                        if hh == 0:
                            pvA_last = t_pv
                        else:
                            pvB_last = t_pv
                norm_toks = []
                for hh, pv_last in ((0, pvA_last), (1, pvB_last)):
                    pvb = 2 + hh
                    t_rc0 = c.op("vector", lambda v, pvb=pvb, hh=hh: v.reciprocal(
                        out=rsf[0:1, hh, :], in_=r.ps[pvb][64:65, :]),
                        waits=[pv_last, getattr(c, "_s3_rs_war", None)], inc="v")
                    t_rc = c.op("scalar", lambda s, hh=hh: s.copy(
                        out=rs[0:1, hh, :], in_=rsf[0:1, hh, :]),
                        waits=[t_rc0, getattr(c, "_s3_rs_war", None)], inc="s")
                    bankb = 4 + hh
                    mm_accum(c, bankb, [(r.onesb[0:1, 0:64], rs[0:1, hh, :])],
                             waits=[t_rc])
                    t_bc = evac(c, bankb, lambda s, bankb=bankb, hh=hh: s.copy(
                        out=bc[:, hh, :], in_=r.ps[bankb][0:64, :]),
                        waits=[getattr(c, "_s3_bc_war", None)])
                    dstp = big[hh * 64:hh * 64 + 64, pr, qsl]
                    t_o = c.op("vector", lambda v, pvb=pvb, hh=hh, dstp=dstp: v.tensor_tensor(
                        out=dstp, in0=r.ps[pvb][0:64, :], in1=bc[:, hh, :], op=OP.mult),
                        waits=[t_bc], inc="v")
                    r.ps_read[pvb] = t_o
                    norm_toks.append(t_o)
                c._s3_rs_war = norm_toks
                c._s3_bc_war = norm_toks
                q_war[qslot] = [pvA_last, pvB_last]
            c._s3_attn_done = norm_toks
            w_war = [t_q, k_toks[-1], v_toks[-1]]
        last_attn = c.sems["v"], c.sems["v"].v
    c.barrier()
    with contextlib.ExitStack() as ph2:
        sb2 = lambda name, shape, dt: ph2.enter_context(nc.sbuf_tensor(name, shape, dt))
        wp = sb2("s3_wp", [P, KC, D], BF16)
        t_wp = c.dma("gpsimd", "wl_wp", lambda g: g.dma_start(
            out=wp[:, :, :], in_=W["WprojT"].rearrange("(k p) n -> p k n", p=P)))
        res_war = [None, None]
        out_war = [None, None]
        for tt in range(NT):
            sl2 = tt % 2
            rt = r.rtile[:, sl2, :]
            t_rx = c.dma("sync", f"res{sl2}", lambda s, rt=rt, tt=tt: s.dma_start(
                out=rt, in_=xin[tt * P:(tt + 1) * P, :]),
                waits=[res_war[sl2]])
            oToks = []
            for half in range(2):
                bank = half
                mm_accum(c, bank,
                         [(big[:, kc, tt * P:(tt + 1) * P], wp[:, kc, half * 512:(half + 1) * 512])
                          for kc in range(KC)],
                         waits=[t_wp, last_attn])
                oToks.append(evac(c, bank, lambda v, half=half, rt=rt, bank=bank:
                                  v.tensor_tensor(out=rt[:, half * 512:(half + 1) * 512],
                                                  in0=r.ps[bank][:, :],
                                                  in1=rt[:, half * 512:(half + 1) * 512],
                                                  op=OP.add),
                                  eng="vector", waits=[t_rx]))
            out_war[sl2] = c.dma("sync", f"xo{sl2}", lambda s, rt=rt, tt=tt: s.dma_start(
                out=xout[tt * P:(tt + 1) * P, :], in_=rt),
                waits=oToks)
            res_war[sl2] = out_war[sl2]



def emit_l1_attn_a(c, stack, xin, xout, W, hcm, hcm_toks, big):
    r = c.r
    nc = c.nc
    DEPTH = 6
    curA = nc.dram_tensor("s5_curA", [NTOK, D], F32, kind="Internal").ap()
    curB = nc.dram_tensor("s5_curB", [NTOK, D], F32, kind="Internal").ap()
    skd = nc.dram_tensor("s5_skd", [NTOK, D], BF16, kind="Internal").ap()
    sqd = nc.dram_tensor("s5_sqd", [NTOK, D], BF16, kind="Internal").ap()
    rrd = nc.dram_tensor("s5_rrd", [1024], F32, kind="Internal").ap()
    bigf = big[:, :, :].rearrange("p a n -> p (a n)")
    M2bf = bigf[:, 0:8192].rearrange("p (k n) -> p k n", k=KC)
    wq = bigf[:, 8192:16384].rearrange("p (k n) -> p k n", k=KC)
    wk = bigf[:, 16384:24576].rearrange("p (k n) -> p k n", k=KC)
    sqhalf = bigf[:, 8192:24576].rearrange("p (a n) -> p a n", a=NT)
    skst = bigf[:, 24576:32768].rearrange("p (a b n) -> p a b n", a=2, b=NT)
    vcmb = bigf[:, 8192:12288].rearrange("p (k n) -> p k n", k=KC)
    outcb = bigf[:, 12288:16384].rearrange("p (k n) -> p k n", k=KC)
    wv = bigf[:, 16384:24576].rearrange("p (k n) -> p k n", k=KC)
    wo = bigf[:, 24576:32768].rearrange("p (k n) -> p k n", k=KC)
    abf = stack.enter_context(nc.sbuf_tensor("s5_abf", [P, 7168], BF16))
    sk_bf = abf[:, 0:2048].rearrange("p (a n) -> p a n", a=2)
    sq_bf = abf[:, 2048:4096].rearrange("p (a n) -> p a n", a=2)
    ktmp = abf[:, 4096:6144].rearrange("p (a n) -> p a n", a=2)
    sbct = abf[:, 6144:7168].rearrange("p (a n) -> p a n", a=2)
    af = stack.enter_context(nc.sbuf_tensor("s5_af", [P, 1040], F32))
    gcols = af[:, 0:8]
    bcols = af[:, 8:16]
    rrowf = af[0:1, 16:1040]
    r.open_rows(stack, nc, 4096)
    grow = r.ralloc(1024)
    brow = r.ralloc(1024)
    borow = r.ralloc(1024)
    rrow2 = r.ralloc(1024).rearrange("a (b n) -> a b n", b=2)

    emit_ln_stream(c, xin, curA)
    c.barrier()
    st = r.stats
    for l in range(DEPTH):
        cur_in = curA if l % 2 == 0 else curB
        cur_out = curB if l % 2 == 0 else curA
        last_iter = (l == DEPTH - 1)
        t_gr = c.dma("gpsimd", "g_row", lambda g, l=l: g.dma_start(
            out=grow, in_=W["gamma"][None, l, :]), waits=[getattr(c, "_s5_gb_war", None)])
        t_br = c.dma("gpsimd", "b_row", lambda g, l=l: g.dma_start(
            out=brow, in_=W["beta"][None, l, :]), waits=[getattr(c, "_s5_gb_war", None)])
        gc_toks = []
        for i in range(KC):
            tk = c.op("tensor", lambda t, i=i: t.transpose(
                r.pt[0][:, 2 * i:2 * i + 1], grow[0:1, i * P:(i + 1) * P], r.identb[0:1, 0:1]),
                waits=[t_gr, c.tok_identb, r.pt_read[0]], inc="t")
            gc_toks.append(tk)
        for i in range(KC):
            tk = c.op("tensor", lambda t, i=i: t.transpose(
                r.pt[0][:, 16 + 2 * i:17 + 2 * i], brow[0:1, i * P:(i + 1) * P], r.identb[0:1, 0:1]),
                waits=[t_br, c.tok_identb], inc="t")
            gc_toks.append(tk)
        r.pt_write[0] = gc_toks[-1]
        gview = r.pt[0][:, 0:16].rearrange("p (a b) -> p a b", b=2)[:, :, 0:1]
        bview = r.pt[0][:, 16:32].rearrange("p (a b) -> p a b", b=2)[:, :, 0:1]
        t_g1 = c.op("vector", lambda v: v.tensor_copy(out=gcols, in_=gview),
                    waits=[gc_toks[-1], getattr(c, "_s5_gc_war", None)], inc="v")
        t_g2 = c.op("vector", lambda v: v.tensor_copy(out=bcols, in_=bview),
                    waits=[gc_toks[-1], getattr(c, "_s5_gc_war", None)], inc="v")
        r.pt_read[0] = [t_g1, t_g2]
        c._s5_gb_war = [t_g1, t_g2]
        t_wq = c.dma("gpsimd", "wl_q", lambda g, l=l: g.dma_start(
            out=wq[:, :, :], in_=W["w_qkv"][l, :, 0:1024].rearrange("(k p) n -> p k n", p=P)),
            waits=[getattr(c, "_s5_wq_war", None)])
        t_wk = c.dma("gpsimd", "wl_k", lambda g, l=l: g.dma_start(
            out=wk[:, :, :], in_=W["w_qkv"][l, :, 1024:2048].rearrange("(k p) n -> p k n", p=P)),
            waits=[getattr(c, "_s5_wk_war", None)])
        hin_toks = emit_ln_prologue(c, cur_in, hcm, gamma_cols=gcols, beta_cols=bcols,
                                    gb_tok=[t_g1, t_g2])
        ktmp_war = [None, None]
        skb_war = [None, None]
        sqb_war = [None, None]
        sc_mm_last = None
        for tt in range(NT):
            sl = tt % 2
            tsl = slice(tt * P, (tt + 1) * P)
            kToks = []
            for half in range(2):
                bank = half
                mm_accum(c, bank, [(hcm[:, kc, tsl], wk[:, kc, half * 512:(half + 1) * 512])
                                   for kc in range(KC)],
                         waits=[t_wk] + hin_toks)
                kToks.append(evac(c, bank, lambda s, sl=sl, half=half, bank=bank: s.copy(
                    out=ktmp[:, sl, half * 512:(half + 1) * 512], in_=r.ps[bank][:, :]),
                    waits=[ktmp_war[sl]]))
            t_mx = c.op("vector", lambda v, sl=sl: v.tensor_reduce(
                out=st[:, 31:32], in_=ktmp[:, sl, :], axis=AX.X, op=OP.max),
                waits=kToks, inc="v")
            t_nx = c.op("vector", lambda v: v.tensor_scalar(
                out=st[:, 32:33], in0=st[:, 31:32], scalar1=-1.0, scalar2=None, op0=OP.mult),
                waits=[t_mx], inc="v")
            t_ex = c.op("scalar", lambda s, sl=sl: s.activation(
                out=ktmp[:, sl, :], in_=ktmp[:, sl, :], func=AF.Exp, bias=st[:, 32:33],
                accum_out=st[:, 33:34]),
                waits=[t_nx], inc="s")
            t_rc = c.op("vector", lambda v: v.reciprocal(out=st[:, 34:35], in_=st[:, 33:34]),
                        waits=[t_ex], inc="v")
            t_sk = c.op("vector", lambda v, sl=sl: v.tensor_scalar(
                out=sk_bf[:, sl, :], in0=ktmp[:, sl, :], scalar1=st[:, 34:35], scalar2=None,
                op0=OP.mult),
                waits=[t_rc, skb_war[sl]], inc="v")
            ktmp_war[sl] = t_sk
            t_skd = c.dma("gpsimd", f"sk{sl}", lambda g, sl=sl, tt=tt: g.dma_start(
                out=skd[tt * P:(tt + 1) * P, :], in_=sk_bf[:, sl, :]),
                waits=[t_sk])
            skb_war[sl] = t_skd
            sqToks = []
            for half in range(2):
                bank = 2 + half
                mm_accum(c, bank, [(hcm[:, kc, tsl], wq[:, kc, half * 512:(half + 1) * 512])
                                   for kc in range(KC)],
                         waits=[t_wq] + hin_toks)
                sqToks.append(evac(c, bank, lambda s, sl=sl, half=half, bank=bank: s.activation(
                    out=sq_bf[:, sl, half * 512:(half + 1) * 512], in_=r.ps[bank][:, :],
                    func=AF.Exp),
                    waits=[sqb_war[sl]]))
            for half in range(2):
                bank = 4 + half
                sc_mm_last = c.op("tensor", lambda t, sl=sl, half=half, bank=bank, tt=tt: t.matmul(
                    r.ps[bank][0:1, :], r.onesb[:, 0:1],
                    sq_bf[:, sl, half * 512:(half + 1) * 512],
                    start=(tt == 0), stop=(tt == NT - 1), skip_group_check=True),
                    waits=[sqToks[half], c.tok_ones] +
                          ([r.ps_read[bank]] if tt == 0 else []), inc="t")
                r.ps_write[bank] = sc_mm_last
            t_sqd = c.dma("gpsimd", f"sq{sl}", lambda g, sl=sl, tt=tt: g.dma_start(
                out=sqd[tt * P:(tt + 1) * P, :], in_=sq_bf[:, sl, :]),
                waits=sqToks + [sc_mm_last])
            sqb_war[sl] = t_sqd
        rc_toks = []
        for half in range(2):
            bank = 4 + half
            t1 = c.op("vector", lambda v, bank=bank, half=half: v.reciprocal(
                out=rrowf[0:1, half * 512:(half + 1) * 512], in_=r.ps[bank][0:1, :]),
                waits=[r.ps_write[bank], getattr(c, "_s5_rrf_war", None)], inc="v")
            r.ps_read[bank] = t1
            rc_toks.append(t1)
        t_rro = c.dma("sync", "rr_o", lambda s: s.dma_start(
            out=rrd[None, :], in_=rrowf), waits=rc_toks)
        c._s5_rrf_war = t_rro
        t_rr = c.dma("gpsimd", "rr_i", lambda g: g.dma_start(
            out=rrow2[0:1, :, :].rearrange("a b n -> a (b n)"), in_=rrd[None, :]),
            waits=[t_rro, getattr(c, "_s5_rr2_war", None)])
        sk_war = [[sc_mm_last], [sc_mm_last]]
        m2_done = None
        for half in range(2):
            hsl = slice(half * 512, (half + 1) * 512)
            t_sq = c.dma("sync", f"sqh{half}", lambda s, hsl=hsl: s.dma_start(
                out=sqhalf[:, :, :], in_=sqd[:, hsl].rearrange("(a p) n -> p a n", p=P)),
                waits=[getattr(c, "_s5_sqh_war", None), sqb_war[0], sqb_war[1], sc_mm_last])
            bank = 2
            mm_accum(c, bank, [(r.onesb[0:1, 0:P], rrow2[0:1, half, :])],
                     waits=[t_rr, c.tok_ones])
            t_bc = evac(c, bank, lambda s, half=half, bank=bank: s.copy(
                out=sbct[:, half, :], in_=r.ps[bank][:, :]),
                waits=[getattr(c, "_s5_bct_war", None)])
            for m in range(KC):
                sks = m % 2
                t_skl = c.dma("gpsimd", f"skl{sks}", lambda g, m=m, sks=sks: g.dma_start(
                    out=skst[:, sks, :, :],
                    in_=skd[:, m * P:(m + 1) * P].rearrange("(a p) n -> p a n", p=P)),
                    waits=[sk_war[sks], skb_war[0], skb_war[1]])
                bank = 3 + sks
                mm_accum(c, bank, [(skst[:, sks, a, :], sqhalf[:, a, :]) for a in range(NT)],
                         waits=[t_skl, t_sq])
                t_m2 = evac(c, bank, lambda v, m=m, half=half, bank=bank: v.tensor_tensor(
                    out=M2bf[:, m, half * 512:(half + 1) * 512], in0=r.ps[bank][:, :],
                    in1=sbct[:, half, :], op=OP.mult),
                    eng="vector", waits=[t_bc, getattr(c, "_s5_m2_war", None)])
                sk_war[sks] = r.ps_write[bank]
                m2_done = t_m2
            c._s5_sqh_war = r.ps_write[bank]
        c._s5_bct_war = m2_done
        t_bo = c.dma("gpsimd", "bo_row", lambda g, l=l: g.dma_start(
            out=borow, in_=W["b_out"][None, l, :]),
            waits=[getattr(c, "_s5_bo_war", None)])
        t_wv = c.dma("gpsimd", "wl_v", lambda g, l=l: g.dma_start(
            out=wv[:, :, :], in_=W["w_qkv"][l, :, 2048:3072].rearrange("(k p) n -> p k n", p=P)),
            waits=[sk_war[0], sk_war[1], c._s5_sqh_war])
        t_wo = c.dma("gpsimd", "wl_o", lambda g, l=l: g.dma_start(
            out=wo[:, :, :], in_=W["w_out"][l, :, :].rearrange("(k p) n -> p k n", p=P)),
            waits=[sk_war[0], sk_war[1], c._s5_sqh_war])
        res_war = getattr(c, "_s5_res_war", [None, None])
        xo_war = getattr(c, "_s5_xo_war", [None, None])
        v_war = None
        oc_war = None
        for jc in range(8):
            jsl = slice(jc * 512, (jc + 1) * 512)
            vToks = []
            for m in range(KC):
                bank = m % 2
                mm_accum(c, bank, [(wv[:, kc, m * P:(m + 1) * P], hcm[:, kc, jsl])
                                   for kc in range(KC)],
                         waits=[t_wv, v_war])
                vToks.append(evac(c, bank, lambda s, m=m, bank=bank: s.copy(
                    out=vcmb[:, m, :], in_=r.ps[bank][:, :]),
                    waits=[m2_done]))
            oToks = []
            for m in range(KC):
                bank = 2 + m % 2
                mm_accum(c, bank, [(M2bf[:, kc, m * P:(m + 1) * P], vcmb[:, kc, :])
                                   for kc in range(KC)],
                         waits=[m2_done, oc_war] + vToks)
                oToks.append(evac(c, bank, lambda s, m=m, bank=bank: s.copy(
                    out=outcb[:, m, :], in_=r.ps[bank][:, :])))
            v_war = oToks[-1]
            for tt4 in range(4):
                tt = jc * 4 + tt4
                sl2 = tt % 2
                rt = r.rtile[:, sl2, :]
                t_rx = c.dma("sync", f"res{sl2}", lambda s, rt=rt, tt=tt, cur_in=cur_in: s.dma_start(
                    out=rt, in_=cur_in[tt * P:(tt + 1) * P, :]),
                    waits=[res_war[sl2]])
                if last_iter:
                    xt2 = r.xtile[:, sl2, :]
                    t_rx2 = c.dma("scalar", f"resx{sl2}", lambda s, xt2=xt2, tt=tt: s.dma_start(
                        out=xt2, in_=xin[tt * P:(tt + 1) * P, :]),
                        waits=[xo_war[sl2], oToks[-1]])
                wToks = []
                last_wmm = None
                for half in range(2):
                    bank = 4 + half
                    parts = [(outcb[:, kc, tt4 * P:(tt4 + 1) * P],
                              wo[:, kc, half * 512:(half + 1) * 512]) for kc in range(KC)]
                    parts.append(bias_free(c, borow, half * 512, (half + 1) * 512, P))
                    last_wmm = mm_accum(c, bank, parts, waits=[t_wo, t_bo, c.tok_ones] + oToks)
                    t_add = evac(c, bank, lambda v, half=half, rt=rt, bank=bank:
                                 v.tensor_tensor(out=rt[:, half * 512:(half + 1) * 512],
                                                 in0=r.ps[bank][:, :],
                                                 in1=rt[:, half * 512:(half + 1) * 512],
                                                 op=OP.add),
                                 eng="vector", waits=[t_rx])
                    if last_iter:
                        t_add = c.op("vector", lambda v, half=half, rt=rt, xt2=xt2:
                                     v.tensor_tensor(out=rt[:, half * 512:(half + 1) * 512],
                                                     in0=rt[:, half * 512:(half + 1) * 512],
                                                     in1=xt2[:, half * 512:(half + 1) * 512],
                                                     op=OP.add),
                                     waits=[t_add, t_rx2], inc="v")
                    wToks.append(t_add)
                oc_war = last_wmm
                dst = xout if last_iter else cur_out
                t_wr = c.dma("sync", f"xo{sl2}", lambda s, rt=rt, tt=tt, dst=dst: s.dma_start(
                    out=dst[tt * P:(tt + 1) * P, :], in_=rt),
                    waits=wToks)
                res_war[sl2] = t_wr
                if last_iter:
                    xo_war[sl2] = wToks
        c._s5_res_war = res_war
        c._s5_xo_war = xo_war
        c._s5_wq_war = oc_war
        c._s5_wk_war = oc_war
        c._s5_gc_war = hin_toks[-1]
        c._s5_m2_war = oc_war
        c._s5_rr2_war = m2_done
        c._s5_bo_war = oc_war
        c.barrier()


emit_l1_attn_a.needs_prologue = False



S6_CAP = 1024


def emit_l1_ffn(c, stack, xin, xout, W, hcm, hcm_toks, big, tag="a"):
    r = c.r
    nc = c.nc
    E = 8
    xf16 = nc.dram_tensor(f"s6_xf_{tag}", [NTOK, D], BF16, kind="Internal").ap()
    mocm = nc.dram_tensor(f"s6_mo_{tag}", [D, NTOK], BF16,
                          kind="ExternalOutput" if globals().get("S6_MO_OUT") else "Internal").ap()
    slotd = nc.dram_tensor(f"s6_sl_{tag}", [NTOK], F32, kind="Internal").ap()
    bigf = big[:, :, :].rearrange("p a n -> p (a n)")
    hHalf = bigf[:, 0:16384].rearrange("p (a n) -> p a n", a=32)
    Gt = bigf[:, 16384:32768].rearrange("p (a n) -> p a n", a=NT)
    scacc = bigf[:, 0:4096].rearrange("p (k n) -> p k n", k=KC)
    xfr = bigf[:, 4096:6144].rearrange("p (a n) -> p a n", a=2)
    sb = lambda name, shape, dt: stack.enter_context(nc.sbuf_tensor(f"{name}_{tag}", shape, dt))
    gwf = sb("s6_gwf", [P, KC, E], F32)
    ghi = sb("s6_ghi", [P, KC, E], BF16)
    glo = sb("s6_glo", [P, KC, E], BF16)
    ltile = sb("s6_lt", [P, NT, E], F32)
    hlob = sb("s6_hlo", [P, 1, D], BF16)
    hlocm = sb("s6_hlc", [P, 1, KC, P], BF16)
    iotaP = sb("s6_iop", [P, E], F32)
    iotaE = sb("s6_ioe", [P, E], F32)
    iotaC = sb("s6_ioc", [P, 1], F32)
    iota512 = sb("s6_i512", [P, 512], mybir.dt.int16)
    slotb = sb("s6_slot", [P, NT], F32)
    slstg = sb("s6_slst", [NT, P], F32)
    slbc = sb("s6_slbc", [P, 512], F32)
    base8 = sb("s6_b8", [1, E], F32)
    rtmp = sb("s6_rt", [P, 20], F32)
    ohbf = sb("s6_ohb", [P, E], BF16)
    gsblk = sb("s6_gs", [P, 4, 512], BF16)
    xgc = sb("s6_xgc", [P, KC, 512], BF16)
    w1b = sb("s6_w1", [P, 2, KC, P], BF16)
    w2b = sb("s6_w2", [P, 2, D], BF16)
    yhf = sb("s6_y", [P, 4, D], BF16)
    b1c = sb("s6_b1c", [P, 32], F32)
    b1s = sb("s6_b1s", [32, P], BF16)
    mzb = sb("s6_mz", [P, 2, D], BF16)
    r.open_rows(stack, nc, 1024 + 16, nf32=16)
    b2row = r.ralloc(1024)
    bhrow = r.ralloc(8)
    blrow = r.ralloc(8)
    gbf = r.rfalloc(8)

    t_gw = c.dma("sync", "gw", lambda s: s.dma_start(
        out=gwf[:, :, :], in_=W["gate_wT"].rearrange("(k p) n -> p k n", p=P)))
    t_gb = c.dma("sync", "gb", lambda s: s.dma_start(out=gbf, in_=W["gate_b"][None, :]))
    t_gh = c.op("scalar", lambda s: s.copy(out=ghi[:, :, :], in_=gwf[:, :, :]),
                waits=[t_gw], inc="s")
    t_gl = c.op("vector", lambda v: v.tensor_tensor(out=glo[:, :, :], in0=gwf[:, :, :],
                                                    in1=ghi[:, :, :], op=OP.subtract),
                waits=[t_gh], inc="v")
    t_bh = c.op("scalar", lambda s: s.copy(out=bhrow, in_=gbf), waits=[t_gb], inc="s")
    t_bl = c.op("vector", lambda v: v.tensor_tensor(out=blrow, in0=gbf, in1=bhrow,
                                                    op=OP.subtract),
                waits=[t_bh], inc="v")
    t_i1 = c.op("gpsimd", lambda g: g.iota(iotaP[:, :], pattern=[[1, E]], base=1000,
                                           channel_multiplier=0,
                                           allow_small_or_imprecise_dtypes=True), inc="g")
    t_i2 = c.op("gpsimd", lambda g: g.iota(iotaE[:, :], pattern=[[1, E]], base=0,
                                           channel_multiplier=0,
                                           allow_small_or_imprecise_dtypes=True), inc="g")
    t_i3 = c.op("gpsimd", lambda g: g.iota(iotaC[:, :], pattern=[[0, 1]], base=0,
                                           channel_multiplier=1,
                                           allow_small_or_imprecise_dtypes=True), inc="g")
    t_i5 = c.op("gpsimd", lambda g: g.iota(iota512[:, :], pattern=[[1, 512]], base=0,
                                           channel_multiplier=0), inc="g")
    t_z = c.op("gpsimd", lambda g: g.memset(mzb[:, :, :], 0.0), inc="g")
    mocm_t = mocm.rearrange("(k p) t -> p k t", p=P)
    zToks = [c.dma("sync", f"moz{(kc * 2 + hf) % 2}", lambda s, kc=kc, hf=hf: s.dma_start(
        out=mocm_t[:, kc, hf * 2048:(hf + 1) * 2048],
        in_=mzb[:, :, :].rearrange("p a n -> p (a n)")), waits=[t_z])
        for kc in range(KC) for hf in range(2)]

    base_bank = 5
    state = {"base_tok": None, "rank_war": None}

    def tile_cb(rec):
        tt = rec["tt"]
        ht = rec["ht"]
        xt = rec["xt"]
        ptb = rec["ptb"]
        t_l1 = c.op("vector", lambda v, xt=xt, ht=ht, rec=rec: v.scalar_tensor_tensor(
            out=hlob[:, 0, :], in0=xt, scalar=rec["rstd"], in1=ht,
            op0=OP.mult, op1=OP.subtract),
            waits=[rec["t_ht"], getattr(c, "_s6_hlo_war", None)], inc="v")
        t_l2 = c.op("vector", lambda v, rec=rec: v.tensor_scalar(
            out=hlob[:, 0, :], in0=hlob[:, 0, :], scalar1=rec["nb"], scalar2=None,
            op0=OP.add),
            waits=[t_l1], inc="v")
        tws = [t_l2, c.tok_identb, r.pt_read[ptb]]
        last = None
        for kc in range(KC):
            last = c.op("tensor", lambda t, kc=kc, ptb=ptb: t.transpose(
                r.pt[ptb][:, kc * P:(kc + 1) * P], hlob[:, 0, kc * P:(kc + 1) * P],
                r.identb[:]),
                waits=tws, inc="t")
        r.pt_write[ptb] = last
        t_hc = c.op("vector", lambda v, ptb=ptb: v.tensor_copy(
            out=hlocm[:, 0, :, :], in_=r.pt[ptb].rearrange("p (k n) -> p k n", k=KC)),
            waits=[last, getattr(c, "_s6_hlc_war", None)], inc="v")
        r.pt_read[ptb] = t_hc
        bank = 4 if tt % 2 == 0 else 3
        tsl = slice(tt * P, (tt + 1) * P)
        parts = [(hcm[:, kc, tsl], ghi[:, kc, :]) for kc in range(KC)]
        parts += [(hlocm[:, 0, kc, :], ghi[:, kc, :]) for kc in range(KC)]
        parts += [(hcm[:, kc, tsl], glo[:, kc, :]) for kc in range(KC)]
        parts += [(r.onesb[0:1, 0:P], bhrow), (r.onesb[0:1, 0:P], blrow)]
        lgmm = mm_accum(c, bank, parts,
                        waits=[t_gl, t_bl, t_hc, rec["t_hcm"], c.tok_ones])
        t_lg = evac(c, bank, lambda s, tt=tt, bank=bank: s.copy(
            out=ltile[:, tt, :], in_=r.ps[bank][:, 0:E]))
        c._s6_hlo_war = last
        c._s6_hlc_war = lgmm
        t_m = c.op("vector", lambda v, tt=tt: v.tensor_reduce(
            out=rtmp[:, 0:1], in_=ltile[:, tt, :], axis=AX.X, op=OP.max),
            waits=[t_lg, state["rank_war"]], inc="v")
        t_eq = c.op("vector", lambda v, tt=tt: v.tensor_scalar(
            out=rtmp[:, 2:10], in0=ltile[:, tt, :], scalar1=rtmp[:, 0:1], scalar2=None,
            op0=OP.is_ge),
            waits=[t_m], inc="v")
        t_w = c.op("vector", lambda v: v.scalar_tensor_tensor(
            out=rtmp[:, 2:10], in0=rtmp[:, 2:10], scalar=-1000.0, in1=iotaP[:, :],
            op0=OP.mult, op1=OP.add),
            waits=[t_eq, t_i1], inc="v")
        t_ei = c.op("vector", lambda v: v.tensor_reduce(
            out=rtmp[:, 1:2], in_=rtmp[:, 2:10], axis=AX.X, op=OP.min),
            waits=[t_w], inc="v")
        t_oh = c.op("vector", lambda v: v.tensor_scalar(
            out=rtmp[:, 2:10], in0=iotaE[:, :], scalar1=rtmp[:, 1:2], scalar2=0.0,
            op0=OP.subtract, op1=OP.is_equal),
            waits=[t_ei, t_i2], inc="v")
        t_ohf = c.op("scalar", lambda s: s.copy(out=rtmp[:, 10:18], in_=rtmp[:, 2:10]),
                     waits=[t_oh], inc="s")
        t_ohc = c.op("vector", lambda v: v.tensor_copy(out=ohbf[:, :], in_=rtmp[:, 2:10]),
                     waits=[t_oh], inc="v")
        rb = 2 if tt % 2 == 0 else 1
        rk_mm = mm_accum(c, rb, [(r.striu[:, :], ohbf[:, :])],
                         waits=[c.tok_striu, t_ohc])
        if state["base_tok"] is not None:
            rk_mm = c.op("tensor", lambda t, rb=rb: t.matmul(
                r.ps[rb][0:P, 0:E], r.onesf[0:1, 0:P], base8[0:1, 0:E],
                start=False, stop=True, skip_group_check=True),
                waits=[state["base_tok"], c.tok_onesf], inc="t")
            r.ps_write[rb] = rk_mm
        bs_mm = c.op("tensor", lambda t, tt=tt: t.matmul(
            r.ps[base_bank][0:1, 0:E], r.onesb[:, 0:1], ohbf[:, :],
            start=(tt == 0), stop=True, skip_group_check=True),
            waits=[t_ohc, c.tok_ones] + ([r.ps_read[base_bank]] if tt == 0 else []), inc="t")
        r.ps_write[base_bank] = bs_mm
        t_bs = c.op("scalar", lambda s: s.copy(out=base8[:, :], in_=r.ps[base_bank][0:1, 0:E]),
                    waits=[bs_mm, t_ohf], inc="s")
        state["base_tok"] = t_bs
        t_rk = evac(c, rb, lambda v, rb=rb: v.tensor_tensor(
            out=rtmp[:, 2:10], in0=r.ps[rb][:, 0:E], in1=rtmp[:, 10:18], op=OP.mult),
            eng="vector", waits=[t_ohf])
        t_rs = c.op("vector", lambda v: v.tensor_reduce(
            out=rtmp[:, 18:19], in_=rtmp[:, 2:10], axis=AX.X, op=OP.add),
            waits=[t_rk], inc="v")
        t_ov = c.op("vector", lambda v: v.tensor_scalar(
            out=rtmp[:, 19:20], in0=rtmp[:, 18:19], scalar1=float(S6_CAP - 1), scalar2=1.0e9,
            op0=OP.is_gt, op1=OP.mult),
            waits=[t_rs], inc="v")
        t_s1 = c.op("vector", lambda v: v.scalar_tensor_tensor(
            out=rtmp[:, 18:19], in0=rtmp[:, 1:2], scalar=float(S6_CAP), in1=rtmp[:, 18:19],
            op0=OP.mult, op1=OP.add),
            waits=[t_ov], inc="v")
        t_s2 = c.op("vector", lambda v: v.tensor_tensor(
            out=rtmp[:, 18:19], in0=rtmp[:, 18:19], in1=rtmp[:, 19:20], op=OP.add),
            waits=[t_s1], inc="v")
        t_s3 = c.op("vector", lambda v, tt=tt: v.tensor_scalar(
            out=slotb[:, tt:tt + 1], in0=rtmp[:, 18:19], scalar1=0.0,
            scalar2=None, op0=OP.add),
            waits=[t_s2, getattr(c, "_s6_slot_war", None)], inc="v")
        state["rank_war"] = t_s3
        return {"ht_war": None, "stats_war": t_l2}

    hcm_toks2 = emit_ln_prologue(c, xin, hcm, also_store=xf16, tile_cb=tile_cb)
    t_st1 = c.op("tensor", lambda t: t.transpose(
        r.ps[0][0:NT, 0:P], slotb[:, :], r.identf[:, :]),
        waits=[state["rank_war"], c.tok_identf, r.ps_read[0]], inc="t")
    r.ps_write[0] = t_st1
    t_st2 = c.op("vector", lambda v: v.tensor_copy(out=slstg[:, :], in_=r.ps[0][0:NT, 0:P]),
                 waits=[t_st1], inc="v")
    r.ps_read[0] = t_st2
    t_st3 = c.dma("sync", "sld", lambda s: s.dma_start(
        out=slotd.rearrange("(a b) -> a b", b=P), in_=slstg[:, :]), waits=[t_st2])
    c.barrier()
    xf_war = [None, None]
    w1_war = [None, None]
    w2_war = [None, None]
    gt_war = None
    h_war = None
    y_war = None
    slbc_war = None
    gs_war = None
    sca_war = None
    acc_war = [None, None]
    for e in range(globals().get("S6_NE", E)):
        t_b1s = c.dma("gpsimd", "b1s", lambda g, e=e: g.dma_start(
            out=b1s[:, :], in_=W["b1"][e, :].rearrange("(a p) -> a p", p=P)),
            waits=[getattr(c, "_s6_b1s_war", None)])
        t_b1t = c.op("tensor", lambda t: t.transpose(
            r.pt[0][:, 0:32], b1s[:, :], r.identb[0:32, 0:32]),
            waits=[t_b1s, c.tok_identb, r.pt_read[0]], inc="t")
        r.pt_write[0] = t_b1t
        t_b1c = c.op("vector", lambda v: v.tensor_copy(
            out=b1c[:, :], in_=r.pt[0][:, 0:32]),
            waits=[t_b1t, getattr(c, "_s6_b1c_war", None)], inc="v")
        r.pt_read[0] = t_b1c
        c._s6_b1s_war = t_b1t
        t_b2 = c.dma("gpsimd", "b2r", lambda g, e=e: g.dma_start(
            out=b2row, in_=W["b2"][None, e, :]),
            waits=[getattr(c, "_s6_b2_war", None)])
        for sh in range(globals().get("S6_SH", 2)):
            sbase = e * S6_CAP + sh * 512
            gtToks = []
            for tt in range(NT):
                gtToks.append(c.op("vector", lambda v, tt=tt, sbase=sbase: v.tensor_scalar(
                    out=Gt[:, tt, :], in0=iota512[:, :], scalar1=float(sbase),
                    scalar2=slotb[:, tt:tt + 1], op0=OP.add, op1=OP.is_equal),
                    waits=[t_i5, state["rank_war"], gt_war], inc="v"))
            if globals().get("S6_STOP_AFTER") == "gt":
                gt_war = gtToks[-1]
                continue
            gaToks = []
            _passes = list(((0, range(0, 2)), (1, range(2, 4)),
                            (2, range(4, 6)), (3, range(6, 8))))
            if globals().get("S6_GA_FAKE") == "one":
                _passes = [(0, range(0, 1))]
            if globals().get("S6_GA_FAKE") == "one1":
                _passes = [(0, range(1, 2))]
            for cpass, crange in _passes[:globals().get("S6_GA_PASSES", 4)]:
                for tt in range(NT):
                    xs = tt % 2
                    t_xf = c.dma(globals().get("S6_XF_Q", "scalar"), f"xf{xs}",
                                 lambda s, tt=tt, xs=xs: s.dma_start(
                        out=xfr[:, xs, :], in_=xf16[tt * P:(tt + 1) * P, :]),
                        waits=[xf_war[xs], h_war, acc_war[tt % 2]])
                    nci = len(crange)
                    for ci, cc in enumerate(crange):
                        bank = ci
                        _rhs = Gt[:, tt, :] if not globals().get("S6_GA_FAKE") else r.onesb[:, 0:512]
                        _lhs = (xfr[:, xs, cc * P:(cc + 1) * P]
                                if globals().get("S6_GA_FAKE") != "lhs" else r.identb[:, :])
                        if globals().get("S6_GA_FAKE") == "nomm":
                            c.op("tensor", lambda t: None,
                                 waits=[t_xf, gtToks[tt]], inc=None)
                        else:
                            c.op("tensor", lambda t, tt=tt, cc=cc, xs=xs, bank=bank, _rhs=_rhs, _lhs=_lhs: t.matmul(
                                r.ps[bank][:, :], _lhs, _rhs,
                                start=(tt == 0), stop=(tt == NT - 1), skip_group_check=True),
                                waits=([t_xf, gtToks[tt]] +
                                       ([r.ps_read[bank]] if tt == 0 else [])),
                                inc="t")
                        if tt == NT - 1:
                            r.ps_write[bank] = (c.sems["t"], c.sems["t"].v)
                    if globals().get("S6_GA_FAKE") == "nomm":
                        c.op("tensor", lambda t: t.nop(), inc="t")
                    xf_war[xs] = (c.sems["t"], c.sems["t"].v)
                if globals().get("S6_GA_FAKE") != "nomm":
                    for ci, cc in enumerate(crange):
                        bank = ci
                        gaToks.append(evac(c, bank, lambda s, cc=cc, bank=bank: s.copy(
                            out=xgc[:, cc, :], in_=r.ps[bank][:, :]),
                            waits=[getattr(c, "_s6_xgc_war", None)]))
            gt_war = (c.sems["t"], c.sems["t"].v)
            if globals().get("S6_DBG_XG") == (e, sh):
                dbgx = nc.dram_tensor(f"s6_dbgx_{tag}", [P, KC, 512], BF16,
                                      kind="ExternalOutput").ap()
                c.dma("gpsimd", "dbgx", lambda g: g.dma_start(
                    out=dbgx[:, :, :], in_=xgc[:, :, :]), waits=gaToks)
            if globals().get("S6_STOP_AFTER") == "gather":
                continue
            heToks = []
            for fm in range(32):
                wsl = fm % 2
                t_w1 = c.dma("gpsimd", f"w1{wsl}", lambda g, e=e, fm=fm, wsl=wsl: g.dma_start(
                    out=w1b[:, wsl, :, :],
                    in_=W["w1T"][e, :, fm * P:(fm + 1) * P].rearrange("(k p) n -> p k n", p=P)),
                    waits=[w1_war[wsl]])
                bank = fm % 2
                mm_accum(c, bank, [(w1b[:, wsl, kc, :], xgc[:, kc, :]) for kc in range(KC)],
                         waits=[t_w1, h_war] + gaToks)
                heToks.append(evac(c, bank, lambda s, fm=fm, bank=bank: s.activation(
                    out=hHalf[:, fm, :], in_=r.ps[bank][:, :],
                    func=AF.Relu, bias=b1c[:, fm:fm + 1], scale=1.0),
                    waits=[t_b1c, acc_war[0], acc_war[1]]))
                w1_war[wsl] = r.ps_write[bank]
            c._s6_xgc_war = r.ps_write[1]
            if globals().get("S6_STOP_AFTER") == "h":
                continue
            if globals().get("S6_DBG_H") == (e, sh):
                dbgh = nc.dram_tensor(f"s6_dbgh_{tag}", [P, 32, 512], F32,
                                      kind="ExternalOutput").ap()
                c.dma("gpsimd", "dbgh", lambda g: g.dma_start(
                    out=dbgh[:, :, :], in_=hHalf[:, :, :]),
                    waits=heToks)
            for sgrp in ((0, 1), (2, 3)):
                for ch in range(2):
                    for fm in range(32):
                        wsl = fm % 2
                        t_w2 = c.dma("gpsimd", f"w2{wsl}", lambda g, e=e, fm=fm, ch=ch, wsl=wsl: g.dma_start(
                            out=w2b[:, wsl, 0:512],
                            in_=W["w2T"][e, fm * P:(fm + 1) * P, ch * 512:(ch + 1) * 512]),
                            waits=[w2_war[wsl]])
                        for si, s_ in enumerate(sgrp):
                            bank = si
                            c.op("tensor", lambda t, fm=fm, s_=s_, bank=bank, wsl=wsl: t.matmul(
                                r.ps[bank][:, :], hHalf[:, fm, s_ * P:(s_ + 1) * P],
                                w2b[:, wsl, 0:512],
                                start=(fm == 0), stop=(fm == 31), skip_group_check=True),
                                waits=(([t_w2, y_war] + heToks + [r.ps_read[bank]])
                                       if fm == 0 else [t_w2]),
                                inc="t")
                            if fm == 31:
                                r.ps_write[bank] = (c.sems["t"], c.sems["t"].v)
                        w2_war[wsl] = (c.sems["t"], c.sems["t"].v)
                    for si, s_ in enumerate(sgrp):
                        bank = si
                        bmm = c.op("tensor", lambda t, bank=bank, ch=ch: t.matmul(
                            r.ps[bank][:, :], r.onesb[0:1, 0:P],
                            b2row[0:1, ch * 512:(ch + 1) * 512],
                            start=False, stop=True, skip_group_check=True),
                            waits=[t_b2, c.tok_ones, r.ps_write[bank]], inc="t")
                        r.ps_write[bank] = bmm
                        evac(c, bank, lambda s, s_=s_, ch=ch, bank=bank: s.copy(
                            out=yhf[:, s_, ch * 512:(ch + 1) * 512], in_=r.ps[bank][:, :]),
                            waits=[sca_war])
            y_done = (c.sems["s"], c.sems["s"].v)
            h_war = w2_war[1]
            if globals().get("S6_STOP_AFTER") == "y":
                continue
            if globals().get("S6_DBG_Y") == (e, sh):
                dbgy = nc.dram_tensor(f"s6_dbgy_{tag}", [512, D], BF16,
                                      kind="ExternalOutput").ap()
                for si4 in range(4):
                    c.dma("sync", "dbgy", lambda s, si4=si4: s.dma_start(
                        out=dbgy[si4 * P:(si4 + 1) * P, :], in_=yhf[:, si4, :]),
                        waits=[y_done])
            for tc in range(8):
                t_sr = c.dma("sync", "slr", lambda s, tc=tc: s.dma_start(
                    out=slbc[0:1, :], in_=slotd[None, tc * 512:(tc + 1) * 512]),
                    waits=[slbc_war, t_st3])
                mm_accum(c, 4, [(r.onesf[0:1, 0:P], slbc[0:1, :])],
                         waits=[t_sr, c.tok_onesf])
                t_sb = evac(c, 4, lambda s: s.copy(out=slbc[:, :], in_=r.ps[4][:, :]),
                            waits=[])
                gsToks = []
                for si in range(4):
                    gsToks.append(c.op("vector", lambda v, si=si, sbase=sbase: v.tensor_scalar(
                        out=gsblk[:, si, :], in0=slbc[:, :], scalar1=iotaC[:, 0:1],
                        scalar2=float(sbase + si * P), op0=OP.subtract, op1=OP.is_equal),
                        waits=[t_sb, t_i3, gs_war], inc="v"))
                gs_war = None
                slbc_war = gsToks[-1]
                scToks = []
                for cc in range(KC):
                    bank = 2 + cc % 2
                    parts = [(yhf[:, si, cc * P:(cc + 1) * P], gsblk[:, si, :])
                             for si in range(4)]
                    mm_accum(c, bank, parts, waits=[y_done] + gsToks)
                    scToks.append(evac(c, bank, lambda s, cc=cc, bank=bank: s.copy(
                        out=scacc[:, cc, :], in_=r.ps[bank][:, :]),
                        waits=[acc_war[0], acc_war[1]]))
                if globals().get("S6_DBG_SC") == (e, sh, tc):
                    dbgs = nc.dram_tensor(f"s6_dbgs_{tag}", [P, KC, 512], F32,
                                          kind="ExternalOutput").ap()
                    c.dma("gpsimd", "dbgs", lambda g: g.dma_start(
                        out=dbgs[:, :, :], in_=scacc[:, :, :]), waits=scToks)
                    dbgg = nc.dram_tensor(f"s6_dbgg_{tag}", [P, 4, 512], F32,
                                          kind="ExternalOutput").ap()
                    c.dma("gpsimd", "dbgg", lambda g: g.dma_start(
                        out=dbgg[:, :, :], in_=gsblk[:, :, :]), waits=gsToks + scToks)
                    dbgb = nc.dram_tensor(f"s6_dbgb_{tag}", [P, 512], F32,
                                          kind="ExternalOutput").ap()
                    c.dma("gpsimd", "dbgb", lambda g: g.dma_start(
                        out=dbgb[:, :], in_=slbc[:, :]), waits=gsToks + scToks)
                t_ac = c.dma("gpsimd", f"acc{tc % 2}", lambda g, tc=tc: g.dma_start(
                    out=mocm_t[:, :, tc * 512:(tc + 1) * 512],
                    in_=scacc[:, :, :], accum_op=OP.add),
                    waits=scToks + zToks)
                acc_war[tc % 2] = t_ac
                gs_war = (c.sems["t"], c.sems["t"].v)
            sca_war = (c.sems["t"], c.sems["t"].v)
            c._s6_b2_war = sca_war
            c._s6_b1c_war = heToks[-1]
    c.barrier()
    mo_war = [None, None]
    res_war = [None, None]
    for tt in range(NT):
        sl2 = tt % 2
        rt = r.rtile[:, sl2, :]
        t_rx = c.dma("sync", f"res{sl2}", lambda s, rt=rt, tt=tt: s.dma_start(
            out=rt, in_=xin[tt * P:(tt + 1) * P, :]),
            waits=[res_war[sl2]])
        t_mo = c.dma("scalar", f"mo{sl2}", lambda s, sl2=sl2, tt=tt: s.dma_start(
            out=mzb[:, sl2, :].rearrange("p (k n) -> p k n", k=KC),
            in_=mocm_t[:, :, tt * P:(tt + 1) * P]),
            waits=[mo_war[sl2]])
        ptb = tt % 2
        last = None
        for kc in range(KC):
            last = c.op("tensor", lambda t, kc=kc, sl2=sl2, ptb=ptb: t.transpose(
                r.pt[ptb][:, kc * P:(kc + 1) * P],
                mzb[:, sl2, :].rearrange("p (k n) -> p k n", k=KC)[:, kc, :],
                r.identb[:]),
                waits=[t_mo, c.tok_identb, r.pt_read[ptb]], inc="t")
        r.pt_write[ptb] = last
        mo_war[sl2] = last
        t_ad = c.op("vector", lambda v, rt=rt, ptb=ptb: v.tensor_tensor(
            out=rt, in0=rt, in1=r.pt[ptb][:, 0:D], op=OP.add),
            waits=[t_rx, last], inc="v")
        r.pt_read[ptb] = t_ad
        t_wr = c.dma("sync", f"xo{sl2}", lambda s, rt=rt, tt=tt: s.dma_start(
            out=xout[tt * P:(tt + 1) * P, :], in_=rt),
            waits=[t_ad])
        res_war[sl2] = t_wr


emit_l1_ffn.needs_prologue = False



class _S7Stream:
    def __init__(self):
        self.Lrow_war = None
        self.brow_war = None
        self.sdx_war = None
        self.Dt_war = None
        self.N_war = None
        self.MT_war = None
        self.P_war = None
        self.G_war = None
        self.Ktm_war = None
        self.W_war = None
        self.U_war = None
        self.Stok = None
        self.Sbf_war = None
        self.Sf_rw = None
        self.ptK_r = None
        self.ptN_r = None


def emit_l1_attn_b(c, stack, xin, xout, W, hcm, hcm_toks, big):
    r = c.r
    nc = c.nc
    H = 16
    HD = 64
    otm = big.rearrange("p k n -> p (k n)").rearrange("p (a n) -> p a n", n=D)
    with contextlib.ExitStack() as ph1:
        sb1 = lambda name, shape, dt: ph1.enter_context(nc.sbuf_tensor(name, shape, dt))
        wqh = sb1("s7_wq", [P, KC, P], BF16)
        wkh = sb1("s7_wk", [P, KC, P], BF16)
        wvh = sb1("s7_wv", [P, KC, P], BF16)
        wfg = wqh[:, :, 0:32]
        qp = sb1("s7_q", [P, NTOK], BF16)
        kp = sb1("s7_k", [P, NTOK], BF16)
        vp = sb1("s7_v", [P, NT, P], BF16)
        lf = sb1("s7_lf", [P, NT, H], F32)
        bfjS = r.scrf.rearrange("p (a h) -> p a h", h=H)
        Lall = sb1("s7_La", [P, NT, H], F32)
        Fall = sb1("s7_Fa", [P, NT, H], F32)
        bFall = sb1("s7_bF", [P, NT, H], F32)
        kkb = sb1("s7_kkb", [P, 512], BF16)
        trif = sb1("s7_trif", [P, P], F32)[:, :]
        rsq2 = sb1("s7_rsq2", [1, 512], F32)
        rsqrow = lf.rearrange("p a h -> p (a h)")[0:1, 0:512]
        Sf = sb1("s7_Sf", [P, HD], F32)
        Sbf = sb1("s7_Sb", [P, HD], BF16)
        st0 = _S7Stream()
        st1 = _S7Stream()
        strm = [st0, st1]
        for hh, S in enumerate(strm):
            S.Lrow = sb1(f"s7_Lr{hh}", [1, P], F32)[:, :]
            S.brow = sb1(f"s7_br{hh}", [1, P], F32)[:, :]
            S.sdx = sb1(f"s7_sdx{hh}", [P, P], F32)[:, :]
            S.Dt = sb1(f"s7_Dt{hh}", [P, P], F32)[:, :]
            S.Nbf = sb1(f"s7_N{hh}", [P, P], BF16)[:, :]
            S.MT = sb1(f"s7_MT{hh}", [P, P], BF16)[:, :]
            S.Pbf = sb1(f"s7_P{hh}", [P, P], BF16)[:, :]
            S.Gt = sb1(f"s7_G{hh}", [P, P], BF16)[:, :]
            S.Ktm = sb1(f"s7_Km{hh}", [P, HD], BF16)[:, :]
            S.Wsb = sb1(f"s7_W{hh}", [P, HD], BF16)[:, :]
            S.Usb = sb1(f"s7_U{hh}", [P, HD], BF16)[:, :]
            S.ecol = sb1(f"s7_ec{hh}", [P, 1], F32)[:, :]
            S.rcol = sb1(f"s7_rc{hh}", [P, 1], F32)[:, :]
            S.rcol2 = sb1(f"s7_rc2{hh}", [P, 1], F32)[:, :]
            S.ecol_war = None
            S.rcol_war = None
            S.rcol2_war = None
            S.banks = (0, 2, 4) if hh == 0 else (1, 3, 5)
        t_wf1 = c.dma("gpsimd", "wl_q", lambda g: g.dma_start(
            out=wfg[:, :, 0:16], in_=W["Wf"].rearrange("(k p) n -> p k n", p=P)))
        t_wf2 = c.dma("gpsimd", "wl_q", lambda g: g.dma_start(
            out=wfg[:, :, 16:32], in_=W["Wgamma"].rearrange("(k p) n -> p k n", p=P)))
        tok_trif = c.op("vector", lambda v: v.tensor_copy(out=trif, in_=r.triub[:]),
                        waits=[c.tok_triub], inc="v")
        sg = r.stats[:, 32:64]
        sg_war = None
        fgL = []
        fgB = []
        for tt in range(NT):
            bank = tt % 2
            tsl = slice(tt * P, (tt + 1) * P)
            mm_accum(c, bank, [(hcm[:, kc, tsl], wfg[:, kc, :]) for kc in range(KC)],
                     waits=[t_wf2, hcm_toks[tt]])
            tsg = evac(c, bank, lambda s, bank=bank: s.activation(
                out=sg, in_=r.ps[bank][0:P, 0:32], func=AF.Sigmoid),
                waits=[sg_war])
            tlf = c.op("scalar", lambda s, tt=tt: s.activation(
                out=lf[:, tt, :], in_=sg[:, 0:16], func=AF.Ln, bias=r.consts[:, 4:5]),
                waits=[tsg, c.tok_tiny], inc="s")
            tbf = c.op("vector", lambda v, tt=tt: v.scalar_tensor_tensor(
                out=bfjS[:, tt, :], in0=sg[:, 16:32], scalar=-1.0, in1=sg[:, 0:16],
                op0=OP.mult, op1=OP.mult),
                waits=[tsg], inc="v")
            sg_war = [tlf, tbf]
            fgL.append(tlf)
            fgB.append(tbf)
        laTok = []
        fTok = []
        bfTok = []
        laMM_last = None
        for cc in range(NT):
            bank = cc % 2
            laMM_last = mm_accum(c, bank, [(trif, lf[:, cc, :])],
                                 waits=[tok_trif, fgL[cc]])
            tLa = evac(c, bank, lambda s, cc=cc, bank=bank: s.copy(
                out=Lall[:, cc, :], in_=r.ps[bank][0:P, 0:16]))
            laTok.append(tLa)
            tF = c.op("scalar", lambda s, cc=cc: s.activation(
                out=Fall[:, cc, :], in_=Lall[:, cc, :], func=AF.Exp),
                waits=[tLa], inc="s")
            fTok.append(tF)
            tbF = c.op("vector", lambda v, cc=cc: v.tensor_tensor(
                out=bFall[:, cc, :], in0=bfjS[:, cc, :], in1=Fall[:, cc, :], op=OP.mult),
                waits=[tF, fgB[cc]], inc="v")
            bfTok.append(tbF)

        def scan_chunk(S, hh, h, cc):
            p0 = hh * HD
            pA, pB, pC = S.banks
            cs = slice(cc * P, (cc + 1) * P)
            strip = cc // 4
            tL = c.op("tensor", lambda t, cc=cc, h=h, pA=pA: t.transpose(
                r.ps[pA][0:1, 0:P], Lall[:, cc, h:h + 1], r.identf[:, :]),
                waits=[laTok[cc], c.tok_identf, r.ps_read[pA]], inc="t")
            r.ps_write[pA] = tL
            yield
            tLr = evac(c, pA, lambda s, S=S, pA=pA: s.copy(
                out=S.Lrow, in_=r.ps[pA][0:1, 0:P]), waits=[S.Lrow_war])
            yield
            bcE = mm_accum(c, pB, [(r.onesf[0:1, 0:P], S.Lrow[0:1, P - 1:P])],
                           waits=[tLr, c.tok_onesf])
            yield
            tEc = evac(c, pB, lambda s, S=S, pB=pB: s.activation(
                out=S.ecol, in_=r.ps[pB][0:P, 0:1], func=AF.Exp),
                waits=[S.ecol_war])
            yield
            tsu = c.op("vector", lambda v, S=S, cc=cc, h=h, pB=pB: v.tensor_tensor(
                out=S.rcol, in0=r.ps[pB][0:P, 0:1], in1=Lall[:, cc, h:h + 1],
                op=OP.subtract),
                waits=[bcE, laTok[cc], S.rcol_war], inc="v")
            r.ps_read[pB] = [tEc, tsu]
            yield
            tRc = c.op("scalar", lambda s, S=S: s.activation(
                out=S.rcol2, in_=S.rcol, func=AF.Exp),
                waits=[tsu, S.rcol2_war], inc="s")
            S.rcol_war = tRc
            yield
            tB = c.op("tensor", lambda t, cc=cc, h=h, pB=pB: t.transpose(
                r.ps[pB][0:1, 0:P], bfjS[:, cc, h:h + 1], r.identf[:, :]),
                waits=[fgB[cc], c.tok_identf, r.ps_read[pB]], inc="t")
            r.ps_write[pB] = tB
            yield
            tBr = evac(c, pB, lambda s, S=S, pB=pB: s.copy(
                out=S.brow, in_=r.ps[pB][0:1, 0:P]), waits=[S.brow_war])
            yield
            bmmD = mm_accum(c, pA, [(r.onesf[0:1, 0:P], S.Lrow)],
                            waits=[tLr, c.tok_onesf])
            yield
            tD1 = evac(c, pA, lambda v, S=S, cc=cc, h=h, pA=pA: v.tensor_scalar(
                out=S.sdx, in0=r.ps[pA][0:P, 0:P], scalar1=Lall[:, cc, h:h + 1],
                scalar2=0.0, op0=OP.subtract, op1=OP.min),
                eng="vector", waits=[S.sdx_war, laTok[cc]])
            yield
            tD2 = c.op("scalar", lambda s, S=S: s.activation(
                out=S.Dt, in_=S.sdx, func=AF.Exp),
                waits=[tD1, S.Dt_war], inc="s")
            yield
            tD3 = c.op("vector", lambda v, S=S: v.tensor_tensor(
                out=S.Dt, in0=S.Dt, in1=trif, op=OP.mult),
                waits=[tD2, tok_trif], inc="v")
            yield
            kk = mm_accum(c, pB, [(kp[p0:p0 + HD, cs], kp[p0:p0 + HD, cs])],
                          waits=[l2tok[hh][strip]])
            yield
            tN1 = evac(c, pB, lambda v, S=S, pB=pB: v.tensor_tensor(
                out=S.sdx, in0=r.ps[pB][0:P, 0:P], in1=S.Dt, op=OP.mult),
                eng="vector", waits=[tD3])
            yield
            bmm2 = mm_accum(c, pA, [(r.onesf[0:1, 0:P], S.brow)],
                            waits=[tBr, c.tok_onesf])
            yield
            tN2 = evac(c, pA, lambda v, S=S, pA=pA: v.tensor_tensor(
                out=S.Nbf, in0=S.sdx, in1=r.ps[pA][0:P, 0:P], op=OP.mult),
                eng="vector", waits=[tN1, S.N_war])
            yield
            tN3 = c.op("vector", lambda v, S=S: v.tensor_tensor(
                out=S.Nbf, in0=S.Nbf, in1=r.striu[:], op=OP.mult),
                waits=[tN2], inc="v")
            yield
            tP0 = c.op("vector", lambda v, S=S: v.tensor_tensor(
                out=S.Pbf, in0=r.identb[:], in1=S.Nbf, op=OP.add),
                waits=[tN3, c.tok_identb, S.P_war], inc="v")
            yield
            ttr0 = c.op("tensor", lambda t, S=S, hh=hh: t.transpose(
                r.pt[hh][:, 128:256], S.Nbf, r.identb[:, :]),
                waits=[tN3, c.tok_identb, S.ptN_r], inc="t")
            yield
            tMT = c.op("scalar", lambda s, S=S, hh=hh: s.copy(
                out=S.MT, in_=r.pt[hh][:, 128:256]),
                waits=[ttr0, S.MT_war], inc="s")
            S.ptN_r = tMT
            yield
            tokM, tokMT, tokP = tN3, tMT, tP0
            mmMT = None
            mmP = None
            for sti in range(1, 7):
                mmMT = mm_accum(c, pA, [(S.Nbf, S.MT)], waits=[tokM, tokMT])
                yield
                mmM = None
                if sti < 6:
                    mmM = mm_accum(c, pB, [(S.MT, S.Nbf)], waits=[tokM, tokMT])
                    yield
                tMTn = evac(c, pA, lambda s, S=S, pA=pA: s.copy(
                    out=S.MT, in_=r.ps[pA][0:P, 0:P]), waits=[mmMT, mmM])
                yield
                if sti < 6:
                    tMn = evac(c, pB, lambda s, S=S, pB=pB: s.copy(
                        out=S.Nbf, in_=r.ps[pB][0:P, 0:P]), waits=[mmMT, mmM])
                    yield
                    tokM = tMn
                mmP = mm_accum(c, pC, [(S.MT, S.Pbf)], waits=[tMTn, tokP])
                yield
                tokP = evac(c, pC, lambda v, S=S, pC=pC: v.tensor_tensor(
                    out=S.Pbf, in0=S.Pbf, in1=r.ps[pC][0:P, 0:P], op=OP.add),
                    eng="vector")
                yield
                tokMT = tMTn
            S.N_war = mmMT
            S.MT_war = mmP
            wm = mm_accum(c, pA, [(kp[p0:p0 + HD, cs], Sbf[p0:p0 + HD, :])],
                          waits=[S.Stok, l2tok[hh][strip]])
            yield
            tW = evac(c, pA, lambda v, S=S, cc=cc, h=h, pA=pA: v.tensor_scalar(
                out=S.Wsb, in0=r.ps[pA][0:P, 0:HD], scalar1=bFall[:, cc, h:h + 1],
                scalar2=None, op0=OP.mult),
                eng="vector", waits=[S.W_war, bfTok[cc]])
            yield
            tW2 = c.op("vector", lambda v, S=S, cc=cc, p0=p0: v.tensor_tensor(
                out=S.Wsb, in0=S.Wsb, in1=vp[:, cc, p0:p0 + HD], op=OP.add),
                waits=[tW, vtok[cc]], inc="v")
            yield
            um = mm_accum(c, pC, [(S.Pbf, S.Wsb)], waits=[tokP, tW2])
            S.P_war = um
            S.W_war = um
            yield
            tU = evac(c, pC, lambda s, S=S, pC=pC: s.copy(
                out=S.Usb, in_=r.ps[pC][0:P, 0:HD]), waits=[S.U_war])
            yield
            gm = mm_accum(c, pA, [(kp[p0:p0 + HD, cs], qp[p0:p0 + HD, cs])],
                          waits=[l2tok[hh][strip], qtok[strip]])
            yield
            tG = evac(c, pA, lambda v, S=S, pA=pA: v.tensor_tensor(
                out=S.Gt, in0=r.ps[pA][0:P, 0:P], in1=S.Dt, op=OP.mult),
                eng="vector", waits=[tD3, S.G_war])
            S.Dt_war = [tN1, tG]
            S.sdx_war = tN2
            yield
            om1 = mm_accum(c, pB, [(qp[p0:p0 + HD, cs], Sbf[p0:p0 + HD, :])],
                           waits=[S.Stok, qtok[strip]])
            yield
            om2 = mm_accum(c, pC, [(S.Gt, S.Usb)], waits=[tG, tU])
            S.G_war = om2
            yield
            osl = otm[:, cc, h * HD:(h + 1) * HD]
            to1 = evac(c, pB, lambda s, osl=osl, cc=cc, h=h, pB=pB: s.activation(
                out=osl, in_=r.ps[pB][0:P, 0:HD], func=AF.Identity,
                scale=Fall[:, cc, h:h + 1]),
                waits=[fTok[cc]])
            yield
            to2 = evac(c, pC, lambda v, osl=osl, pC=pC: v.tensor_tensor(
                out=osl, in0=osl, in1=r.ps[pC][0:P, 0:HD], op=OP.add),
                eng="vector", waits=[to1])
            yield
            ktr = c.op("tensor", lambda t, cs=cs, p0=p0, hh=hh: t.transpose(
                r.pt[hh][:, 0:HD], kp[p0:p0 + HD, cs],
                r.identb[p0:p0 + HD, p0:p0 + HD]),
                waits=[l2tok[hh][strip], c.tok_identb, S.ptK_r], inc="t")
            yield
            tKtm = c.op("vector", lambda v, S=S, hh=hh: v.tensor_scalar(
                out=S.Ktm, in0=r.pt[hh][:, 0:HD], scalar1=S.rcol2,
                scalar2=None, op0=OP.mult),
                waits=[ktr, tRc, S.Ktm_war], inc="v")
            S.ptK_r = tKtm
            S.rcol2_war = tKtm
            yield
            sm = mm_accum(c, pA, [(S.Ktm, S.Usb)], waits=[tKtm, tU])
            S.Ktm_war = sm
            S.U_war = [om2, sm]
            yield
            tS1 = c.op("vector", lambda v, S=S, p0=p0: v.tensor_scalar(
                out=Sf[p0:p0 + HD, :], in0=Sf[p0:p0 + HD, :],
                scalar1=S.ecol[p0:p0 + HD, 0:1], scalar2=None, op0=OP.mult),
                waits=[S.Sf_rw, tEc], inc="v")
            S.ecol_war = tS1
            yield
            tS2 = c.op("vector", lambda v, p0=p0, pA=pA: v.tensor_tensor(
                out=Sf[p0:p0 + HD, :], in0=Sf[p0:p0 + HD, :],
                in1=r.ps[pA][0:HD, 0:HD], op=OP.add),
                waits=[tS1, sm], inc="v")
            r.ps_read[pA] = tS2
            yield
            tSb = c.op("scalar", lambda s, p0=p0: s.copy(
                out=Sbf[p0:p0 + HD, :], in_=Sf[p0:p0 + HD, :]),
                waits=[tS2, wm, om1], inc="s")
            S.Stok = tSb
            S.Sf_rw = tSb
            S.Sbf_war = [wm, om1]
            S.Lrow_war = bmmD
            S.brow_war = bmm2

        w_war = None
        qkv_war = None
        for hp in range(H // 2):
            csl = slice(hp * P, (hp + 1) * P)
            if hp == 0:
                w_war = (c.sems["t"], c.sems["t"].v)
            t_wq = c.dma("gpsimd", "wl_q", lambda g, csl=csl: g.dma_start(
                out=wqh[:, :, :], in_=W["Wq"][:, csl].rearrange("(k p) n -> p k n", p=P)),
                waits=[w_war])
            t_wk = c.dma("gpsimd", "wl_k", lambda g, csl=csl: g.dma_start(
                out=wkh[:, :, :], in_=W["Wk"][:, csl].rearrange("(k p) n -> p k n", p=P)),
                waits=[w_war])
            t_wv = c.dma("gpsimd", "wl_v", lambda g, csl=csl: g.dma_start(
                out=wvh[:, :, :], in_=W["Wv"][:, csl].rearrange("(k p) n -> p k n", p=P)),
                waits=[w_war])
            qtok = []
            kev = []
            for ncol in range(8):
                nsl = slice(ncol * 512, (ncol + 1) * 512)
                hw = hcm_toks[4 * ncol:4 * ncol + 4]
                bank = ncol % 2
                mm_accum(c, bank, [(wqh[:, kc, :], hcm[:, kc, nsl]) for kc in range(KC)],
                         waits=[t_wq, qkv_war] + hw)
                qtok.append(evac(c, bank, lambda s, nsl=nsl, bank=bank: s.activation(
                    out=qp[:, nsl], in_=r.ps[bank][0:P, 0:512], func=AF.Silu),
                    waits=[qkv_war]))
                bank2 = 2 + ncol % 2
                mm_accum(c, bank2, [(wkh[:, kc, :], hcm[:, kc, nsl]) for kc in range(KC)],
                         waits=[t_wk, qkv_war] + hw)
                kev.append(evac(c, bank2, lambda s, nsl=nsl, bank2=bank2: s.activation(
                    out=kp[:, nsl], in_=r.ps[bank2][0:P, 0:512], func=AF.Silu),
                    waits=[qkv_war]))
            vtok = []
            for tt in range(NT):
                tsl = slice(tt * P, (tt + 1) * P)
                mm_accum(c, 4, [(hcm[:, kc, tsl], wvh[:, kc, :]) for kc in range(KC)],
                         waits=[t_wv, qkv_war, hcm_toks[tt]])
                vtok.append(evac(c, 4, lambda s, tt=tt: s.activation(
                    out=vp[:, tt, :], in_=r.ps[4][0:P, 0:P], func=AF.Silu),
                    waits=[qkv_war]))
            l2tok = [[], []]
            kkb_war = None
            ln_prev = None
            rs_prev = None
            for hh in range(2):
                p0 = hh * HD
                for ncol in range(8):
                    nsl = slice(ncol * 512, (ncol + 1) * 512)
                    tsq = c.op("scalar", lambda s, p0=p0, nsl=nsl: s.activation(
                        out=kkb[p0:p0 + HD, :], in_=kp[p0:p0 + HD, nsl], func=AF.Square),
                        waits=[kev[ncol], kkb_war], inc="s")
                    smm = mm_accum(c, 5, [(r.onesb[p0:p0 + HD, 0:1], kkb[p0:p0 + HD, 0:512])],
                                   waits=[tsq, c.tok_ones])
                    tln = evac(c, 5, lambda s: s.activation(
                        out=rsqrow, in_=r.ps[5][0:1, 0:512], func=AF.Ln,
                        bias=r.consts[0:1, 5:6]),
                        waits=[ln_prev, c.tok_t24, laMM_last])
                    trs = c.op("scalar", lambda s: s.activation(
                        out=rsq2[0:1, :], in_=rsqrow, func=AF.Exp, scale=-0.5),
                        waits=[tln, rs_prev], inc="s")
                    ln_prev = trs
                    bcm = mm_accum(c, 5, [(r.onesf[0:1, 0:P], rsq2[0:1, 0:512])],
                                   waits=[trs, c.tok_onesf])
                    rs_prev = bcm
                    tbc = evac(c, 5, lambda v: v.tensor_copy(
                        out=kkb[:, :], in_=r.ps[5][0:P, 0:512]),
                        eng="vector", waits=[smm, kkb_war])
                    tsc = c.op("vector", lambda v, p0=p0, nsl=nsl: v.tensor_tensor(
                        out=kp[p0:p0 + HD, nsl], in0=kp[p0:p0 + HD, nsl],
                        in1=kkb[p0:p0 + HD, :], op=OP.mult),
                        waits=[tbc], inc="v")
                    kkb_war = tsc
                    l2tok[hh].append(tsc)
            for hh, S in enumerate(strm):
                p0 = hh * HD
                tz = c.op("gpsimd", lambda g, p0=p0: g.memset(Sf[p0:p0 + HD, :], 0.0),
                          waits=[S.Sf_rw], inc="g")
                tzb = c.op("gpsimd", lambda g, p0=p0: g.memset(Sbf[p0:p0 + HD, :], 0.0),
                           waits=[S.Sbf_war, S.Stok], inc="g")
                S.Sf_rw = tz
                S.Stok = tzb
            for cc in range(NT):
                gens = [scan_chunk(strm[hh], hh, hp * 2 + hh, cc) for hh in range(2)]
                alive = [True, True]
                while alive[0] or alive[1]:
                    for gi in range(2):
                        if alive[gi]:
                            try:
                                next(gens[gi])
                            except StopIteration:
                                alive[gi] = False
            w_war = (c.sems["t"], c.sems["t"].v)
            qkv_war = [(c.sems["t"], c.sems["t"].v), (c.sems["v"], c.sems["v"].v)]
    c.barrier()
    with contextlib.ExitStack() as ph2:
        sb2 = lambda name, shape, dt: ph2.enter_context(nc.sbuf_tensor(name, shape, dt))
        wg1 = sb2("s7_wg1", [P, KC, HD], BF16)
        wg2 = sb2("s7_wg2", [HD, D], BF16)
        wo = sb2("s7_wo", [P, KC, D], BF16)
        z1 = sb2("s7_z1", [HD, NTOK], BF16)
        gtile = sb2("s7_gt", [P, 2, D], BF16)
        nwbc = sb2("s7_nw", [P, D], BF16)
        gcm = sb2("s7_gcm", [P, KC, P], BF16)
        r.open_rows(ph2, nc, 1024)
        nwrow, _ = load_row_bf(c, W["norm_w"], D)
        t_nw = c.rows_tok
        t_g1 = c.dma("gpsimd", "wl_g1", lambda g: g.dma_start(
            out=wg1[:, :, :], in_=W["Wg1"].rearrange("(k p) n -> p k n", p=P)))
        t_g2 = c.dma("gpsimd", "wl_g2", lambda g: g.dma_start(
            out=wg2[:, :], in_=W["Wg2"][:, :]))
        t_wo = c.dma("gpsimd", "wl_o", lambda g: g.dma_start(
            out=wo[:, :, :], in_=W["Wo"].rearrange("(k p) n -> p k n", p=P)))
        nw_toks = []
        for half in range(2):
            bank = 4 + half
            hsl = slice(half * 512, (half + 1) * 512)
            mm_accum(c, bank, [(r.onesb[0:1, 0:P], nwrow[0:1, hsl])],
                     waits=[t_nw, c.tok_ones])
            nw_toks.append(evac(c, bank, lambda s, hsl=hsl, bank=bank: s.copy(
                out=nwbc[:, hsl], in_=r.ps[bank][0:P, 0:512])))
        z1_toks = []
        for ncol in range(8):
            nsl = slice(ncol * 512, (ncol + 1) * 512)
            bank = ncol % 2
            mm_accum(c, bank, [(wg1[:, kc, :], hcm[:, kc, nsl]) for kc in range(KC)],
                     waits=[t_g1])
            z1_toks.append(evac(c, bank, lambda s, nsl=nsl, bank=bank: s.copy(
                out=z1[:, nsl], in_=r.ps[bank][0:HD, 0:512])))
        st = r.stats
        res_war = [None, None]
        g_war = [None, None]
        gc_war = None
        rx_war = [None, None]
        st_war = None
        for tt in range(NT):
            sl2 = tt % 2
            tsl = slice(tt * P, (tt + 1) * P)
            gToks = []
            for half in range(2):
                bank = 2 + half
                hsl = slice(half * 512, (half + 1) * 512)
                mm_accum(c, bank, [(z1[:, tt * P:(tt + 1) * P], wg2[:, hsl])],
                         waits=[t_g2, z1_toks[tt // 4]])
                gToks.append(evac(c, bank, lambda s, hsl=hsl, sl2=sl2, bank=bank: s.activation(
                    out=gtile[:, sl2, hsl], in_=r.ps[bank][0:P, 0:512], func=AF.Sigmoid),
                    waits=[g_war[sl2]]))
            rt = r.rtile[:, sl2, :]
            t_og = c.op("vector", lambda v, tt=tt, rt=rt, sl2=sl2: v.tensor_tensor(
                out=rt, in0=otm[:, tt, :], in1=gtile[:, sl2, :], op=OP.mult),
                waits=gToks + [res_war[sl2]], inc="v")
            bn = st[:, 0:12]
            mv = st[:, 12:14]
            t1 = c.op("vector", lambda v, rt=rt, bn=bn: v.bn_stats(out=bn[:, 0:6], in_=rt[:, 0:512]),
                      waits=[t_og], inc="v")
            t2 = c.op("vector", lambda v, rt=rt, bn=bn: v.bn_stats(out=bn[:, 6:12], in_=rt[:, 512:1024]),
                      waits=[t_og], inc="v")
            t3 = c.op("vector", lambda v, bn=bn, mv=mv: v.bn_aggr(out=mv, in_=bn),
                      waits=[t1, t2], inc="v")
            t4 = c.op("scalar", lambda s, mv=mv: s.activation(
                out=st[:, 14:15], in_=mv[:, 1:2], func=AF.Ln, bias=r.consts[:, 0:1], scale=1.0),
                waits=[t3, c.tok_eps, st_war], inc="s")
            t5 = c.op("scalar", lambda s: s.activation(
                out=st[:, 15:16], in_=st[:, 14:15], func=AF.Exp, scale=-0.5),
                waits=[t4], inc="s")
            t6 = c.op("vector", lambda v, mv=mv: v.scalar_tensor_tensor(
                out=st[:, 16:17], in0=mv[:, 0:1], scalar=-1.0, in1=st[:, 15:16],
                op0=OP.mult, op1=OP.mult),
                waits=[t5], inc="v")
            st_war = t6
            t7 = c.op("scalar", lambda s, rt=rt, sl2=sl2: s.activation(
                out=gtile[:, sl2, :], in_=rt, func=AF.Identity, scale=st[:, 15:16],
                bias=st[:, 16:17]),
                waits=[t6, t_og], inc="s")
            res_war[sl2] = t7
            t8 = c.op("vector", lambda v, sl2=sl2: v.tensor_tensor(
                out=gtile[:, sl2, :], in0=gtile[:, sl2, :], in1=nwbc[:, :], op=OP.mult),
                waits=[t7] + nw_toks, inc="v")
            ptb = tt % 2
            tlast = None
            for kc in range(KC):
                tlast = c.op("tensor", lambda t, kc=kc, sl2=sl2, ptb=ptb: t.transpose(
                    r.pt[ptb][:, kc * P:(kc + 1) * P], gtile[:, sl2, kc * P:(kc + 1) * P],
                    r.identb[:]),
                    waits=[t8, c.tok_identb, r.pt_read[ptb]], inc="t")
            r.pt_write[ptb] = tlast
            g_war[sl2] = tlast
            t_gc = c.op("vector", lambda v, ptb=ptb: v.tensor_copy(
                out=gcm[:, :, :], in_=r.pt[ptb].rearrange("p (k n) -> p k n", k=KC)),
                waits=[tlast, gc_war], inc="v")
            r.pt_read[ptb] = t_gc
            t_rx = c.dma("sync", f"res{sl2}", lambda s, sl2=sl2, tsl=tsl: s.dma_start(
                out=r.xtile[:, sl2, :], in_=xin[tsl, :]),
                waits=[rx_war[sl2]])
            oT = []
            mo_toks = []
            for half in range(2):
                bank = half
                hsl = slice(half * 512, (half + 1) * 512)
                momm = mm_accum(c, bank, [(gcm[:, kc, :], wo[:, kc, hsl])
                                          for kc in range(KC)],
                                waits=[t_wo, t_gc])
                mo_toks.append(momm)
                oT.append(evac(c, bank, lambda v, hsl=hsl, sl2=sl2, bank=bank: v.tensor_tensor(
                    out=r.xtile[:, sl2, hsl], in0=r.ps[bank][0:P, 0:512],
                    in1=r.xtile[:, sl2, hsl], op=OP.add),
                    eng="vector", waits=[t_rx]))
            gc_war = mo_toks
            t_wr = c.dma("sync", f"xo{sl2}", lambda s, sl2=sl2, tsl=tsl: s.dma_start(
                out=xout[tsl, :], in_=r.xtile[:, sl2, :]),
                waits=oT)
            rx_war[sl2] = t_wr



def kernel(**inputs):
    ins = {k: np.asarray(v) for k, v in inputs.items()}
    T = lambda a: np.ascontiguousarray(np.asarray(a).T)
    W1 = {"WqT": T(ins["L0_attn_a_Wq"]), "WkT": T(ins["L0_attn_a_Wk"]),
          "WvT": T(ins["L0_attn_a_Wv"]), "WuT": T(ins["L0_attn_a_Wu"]),
          "WoT": T(ins["L0_attn_a_Wo"]),
          "bq": ins["L0_attn_a_bq"], "bk": ins["L0_attn_a_bk"],
          "bv": ins["L0_attn_a_bv"], "bu": ins["L0_attn_a_bu"],
          "bo": ins["L0_attn_a_bo"], "scale": ins["L0_attn_a_scale"]}
    W2 = {"U1": ins["L0_ffn_U1"], "V1": ins["L0_ffn_V1"], "b1": ins["L0_ffn_b1"],
          "U2": ins["L0_ffn_U2"], "V2": ins["L0_ffn_V2"], "b2": ins["L0_ffn_b2"]}
    W3 = {"WqkvT": T(ins["L0_attn_b_w_qkv"]), "WprojT": T(ins["L0_attn_b_w_proj"])}
    W5 = {"w_qkv": ins["L1_attn_a_w_qkv"], "w_out": ins["L1_attn_a_w_out"],
          "b_out": ins["L1_attn_a_b_out"], "gamma": ins["L1_attn_a_gamma"],
          "beta": ins["L1_attn_a_beta"]}
    W6 = {"gate_wT": T(ins["L1_ffn_gate_w"]), "gate_b": ins["L1_ffn_gate_b"],
          "w1T": np.ascontiguousarray(ins["L1_ffn_w1"].transpose(0, 2, 1)),
          "b1": ins["L1_ffn_b1"],
          "w2T": np.ascontiguousarray(ins["L1_ffn_w2"].transpose(0, 2, 1)),
          "b2": ins["L1_ffn_b2"]}
    W7 = {n: ins["L1_attn_b_" + n]
          for n in ("Wq", "Wk", "Wv", "Wgamma", "Wf", "Wg1", "Wg2", "Wo", "norm_w")}
    order = [(emit_l0_attn_a, W1), (emit_l0_ffn, W2), (emit_l0_attn_b, W3),
             (emit_l0_ffn, W2), (emit_l1_attn_a, W5), (emit_l1_ffn, W6),
             (emit_l1_attn_b, W7), (emit_l1_ffn, W6)]
    xs = np.ascontiguousarray(ins["L0_attn_a_x"].astype(np.float32))
    for emit, Wd in order:
        specs = [(k, v.shape) for k, v in Wd.items()]
        nc = build_sublayer_program(emit, specs)
        in_maps = []
        for core in range(N_CORES):
            m = dict(Wd)
            m["x0"] = xs[core]
            in_maps.append(m)
        res = run_bass_kernel_spmd(nc, in_maps, core_ids=list(range(N_CORES)))
        xs = np.stack([res.results[i]["y"] for i in range(N_CORES)], axis=0)
    return xs.astype(np.float32)
